# version 09
# baseline (speedup 1.0000x reference)
import numpy as np
from contextlib import ExitStack
import concourse.bass as bass
import concourse.mybir as mybir
from concourse.bass_utils import run_bass_kernel_spmd
import ml_dtypes

F32 = mybir.dt.float32
BF16 = mybir.dt.bfloat16
I32 = mybir.dt.int32
ALU = mybir.AluOpType
AFT = mybir.ActivationFunctionType
AXX = mybir.AxisListType.X
AP = bass.AP

N_CORES = 8
S, D = 2048, 1024
NT = S // 128
NH, HD = 16, 64
HPG = 4
NHG = NH // HPG
DFF = 4096
NEG = -1.0e30
R_SEL = 13
SEL_DELTA0 = 0.13

N_STAGES = 4
TAPS = []

ENGS = ("tensor", "vector", "scalar", "gpsimd", "sync")
DMAQ = {"dsy": "sync", "dge": "gpsimd", "dse": "scalar"}


_UNIQ = [0]


def uname(nm):
    _UNIQ[0] += 1
    return f"{nm}_u{_UNIQ[0]}"


def bname(x):
    return x if isinstance(x, str) else x.tensor.name


def nc_sem(nc, name):
    return nc.semaphore(name)


class Prog:
    def __init__(self, nc, es):
        self.nc = nc
        self.es = es
        self.semh = {}
        for e in ENGS:
            self.semh[e] = es.enter_context(nc.semaphore(f"sp_{e}"))
        self.count = {e: 0 for e in ENGS}
        self.dcount = {}
        self.ops = {e: [] for e in ENGS}
        self.buf = {}
        self.watermark = {e: {} for e in ENGS}
        self.covered = {}

    def _dma_role(self, name):
        base = name.split("_u")[0]
        role = "dma_" + base
        if role not in self.semh:
            self.semh[role] = self.es.enter_context(nc_sem(self.nc, role))
            self.dcount[role] = 0
        return role

    def _b(self, name):
        return self.buf.setdefault(name, dict(w=None, r=[]))

    def _edges(self, reads, writes, waw):
        ed = []
        for nm in reads:
            b = self._b(nm)
            if b["w"] is not None:
                ed.append(b["w"])
        for nm in writes:
            b = self._b(nm)
            if waw and b["w"] is not None:
                ed.append(b["w"])
            ed.extend(b["r"])
        return ed

    def _filter(self, eng, ed):
        m = {}
        for s, v in ed:
            if m.get(s, -1) < v:
                m[s] = v
        w = self.watermark[eng]
        out = []
        for s, v in sorted(m.items()):
            if self.covered.get(s, 0) >= v or w.get(s, 0) >= v:
                continue
            out.append((s, v))
            w[s] = v
        return out

    def _update(self, reads, writes, ident):
        for nm in writes:
            b = self._b(nm)
            b["w"] = ident
            b["r"] = []
        for nm in reads:
            self._b(nm)["r"].append(ident)

    def op(self, eng, emit, reads=(), writes=(), waw=True):
        reads = [bname(a) for a in reads]
        writes = [bname(a) for a in writes]
        waits = self._filter(eng, self._edges(reads, writes, waw))
        self.count[eng] += 1
        ident = (eng, self.count[eng])
        self.ops[eng].append((waits, emit, (eng, 1)))
        self._update(reads, writes, ident)
        return ident

    def dma(self, q, out_ap, in_ap):
        eng = DMAQ[q]
        role = self._dma_role(bname(out_ap))
        reads = [bname(in_ap)]
        writes = [bname(out_ap)]
        waits = self._filter(eng, self._edges(reads, writes, True))
        self.dcount[role] += 1
        ident = (role, 16 * self.dcount[role])
        self.ops[eng].append(
            (waits, (lambda E, o=out_ap, i=in_ap: E.dma_start(out=o, in_=i)),
             (role, 16)))
        self._update(reads, writes, ident)
        return ident

    def wait_all_dma(self, eng):
        ed = [(r, 16 * n) for r, n in self.dcount.items() if n > 0]
        waits = self._filter(eng, ed)
        if waits:
            self.ops[eng].append((waits, None, None))

    def flush_block(self):
        nc = self.nc
        if not any(self.ops[e] for e in ENGS):
            return
        if not hasattr(self, "_bar_counts"):
            self._bar_counts = {}
        targets = []
        for e in ENGS:
            if self.count[e] > self._bar_counts.get(e, 0):
                targets.append((e, self.count[e]))
        for r, n in self.dcount.items():
            if 16 * n > self._bar_counts.get(r, 0):
                targets.append((r, 16 * n))
        for e in ENGS:
            waits = [(s, v) for (s, v) in targets]
            if waits:
                self.ops[e].append((waits, None, None))
        for s, v in targets:
            self._bar_counts[s] = v
        with nc.Block() as blk:
            for eng in ENGS:
                if not self.ops[eng]:
                    continue

                def mk(lst):
                    def emit_all(E):
                        for waits, emitf, inc in lst:
                            for (s, v) in waits:
                                E.wait_ge(self.semh[s], v)
                            if emitf is not None:
                                ins = emitf(E)
                                if inc is not None:
                                    ins.then_inc(self.semh[inc[0]], inc[1])
                    return emit_all
                getattr(blk, eng)(mk(self.ops[eng]))
        self.ops = {e: [] for e in ENGS}
        for e in ENGS:
            self.covered[e] = self.count[e]
        for r, n in self.dcount.items():
            self.covered[r] = 16 * n
        self.watermark = {e: {} for e in ENGS}



def e_copy(P, eng, dst, src):
    if eng == "scalar":
        return P.op(eng, lambda E, d=dst, s=src: E.copy(out=d, in_=s),
                    reads=[src], writes=[dst])
    return P.op(eng, lambda E, d=dst, s=src: E.tensor_copy(out=d, in_=s),
                reads=[src], writes=[dst])


def e_mul(P, dst, src, c):
    return P.op("scalar", lambda E, d=dst, s=src, m=c: E.mul(out=d, in_=s, mul=m),
                reads=[src], writes=[dst])


def e_mm(P, psum, lhsT, rhs, start, stop):
    return P.op("tensor",
                lambda E, o=psum, l=lhsT, r=rhs, st=start, sp=stop:
                E.matmul(o, l, r, start=st, stop=sp),
                reads=[lhsT, rhs], writes=[psum], waw=start)


def e_trans(P, psum, src, ident):
    return P.op("tensor", lambda E, o=psum, s=src, i=ident: E.transpose(o, s, i),
                reads=[src, ident], writes=[psum])


def e_ts(P, dst, src, s1, s2, op0, op1=None, accum=None, eng="vector"):
    def em(E, d=dst, s=src, a=s1, b=s2, o0=op0, o1=op1, ac=accum):
        kw = {}
        if o1 is not None:
            kw["op1"] = o1
        if ac is not None:
            kw["accum_out"] = ac
        return E.tensor_scalar(out=d, in0=s, scalar1=a, scalar2=b, op0=o0, **kw)
    reads = [src] + [x for x in (s1, s2) if isinstance(x, AP)]
    writes = [dst] + ([accum] if accum is not None else [])
    return P.op(eng, em, reads=reads, writes=writes)


def e_tt(P, dst, a, b, op, eng="vector"):
    return P.op(eng, lambda E, d=dst, x=a, y=b, o=op:
                E.tensor_tensor(out=d, in0=x, in1=y, op=o),
                reads=[a, b], writes=[dst])


def e_stt(P, dst, in0, scalar, in1, op0, op1, accum=None):
    def em(E, d=dst, i0=in0, sc=scalar, i1=in1, o0=op0, o1=op1, ac=accum):
        kw = {"accum_out": ac} if ac is not None else {}
        return E.scalar_tensor_tensor(out=d, in0=i0, scalar=sc, in1=i1,
                                      op0=o0, op1=o1, **kw)
    reads = [in0, in1] + ([scalar] if isinstance(scalar, AP) else [])
    writes = [dst] + ([accum] if accum is not None else [])
    return P.op("vector", em, reads=reads, writes=writes)


def e_red(P, dst, src, op, absval=False):
    def em(E, d=dst, s=src, o=op, av=absval):
        kw = {"apply_absolute_value": True} if av else {}
        return E.tensor_reduce(out=d, in_=s, axis=AXX, op=o, **kw)
    return P.op("vector", em, reads=[src], writes=[dst])


def e_act(P, dst, src, func, bias=None, scale=None, accum=None):
    def em(E, d=dst, s=src, f=func, b=bias, sc=scale, ac=accum):
        kw = {}
        if b is not None:
            kw["bias"] = b
        if sc is not None:
            kw["scale"] = sc
        if ac is not None:
            kw["accum_out"] = ac
        return E.activation(out=d, in_=s, func=f, **kw)
    reads = [src] + [x for x in (bias, scale) if isinstance(x, AP)]
    writes = [dst] + ([accum] if accum is not None else [])
    return P.op("scalar", em, reads=reads, writes=writes)


def e_max8(P, dst, src):
    return P.op("vector", lambda E, d=dst, s=src: E.max(out=d, in_=s),
                reads=[src], writes=[dst])


def e_recip(P, dst, src):
    return P.op("vector", lambda E, d=dst, s=src: E.reciprocal(out=d, in_=s),
                reads=[src], writes=[dst])


def e_sqrt(P, dst, src):
    return P.op("scalar", lambda E, d=dst, s=src: E.sqrt(out=d, in_=s),
                reads=[src], writes=[dst])


def e_memset(P, dst, val):
    return P.op("vector", lambda E, d=dst, v=val: E.memset(d, v),
                reads=[], writes=[dst])



def bn_meanvar(P, x_ap, scr, mcol, vcol):
    w = x_ap.shape[-1]
    k = (w + 511) // 512
    for j in range(k):
        P.op("vector", lambda E, d=scr[:, 6 * j:6 * j + 6],
             s=x_ap[:, j * 512:min((j + 1) * 512, w)]: E.bn_stats(out=d, in_=s),
             reads=[x_ap], writes=[scr])
    P.op("vector", lambda E, d=scr[:, mcol:vcol + 1], s=scr[:, 0:6 * k]:
         E.bn_aggr(out=d, in_=s), reads=[scr], writes=[scr])


def layernorm_inplace(P, x_ap, scr):
    bn_meanvar(P, x_ap, scr, 50, 51)
    e_ts(P, scr[:, 52:53], scr[:, 51:52], 1e-5, None, ALU.add)
    e_rsqrt_newton(P, scr[:, 56:57], scr[:, 52:53], scr, 57)
    e_ts(P, x_ap, x_ap, scr[:, 50:51], scr[:, 56:57], ALU.subtract, ALU.mult)


def rope_apply(P, dst, src, cos_i, sin_i, nheads, scrA, scrB):
    hw = nheads * 32
    r4 = lambda a: a.rearrange("p (h d two) -> p h d two", two=2, d=32)
    se = r4(src)[:, :, :, 0:1]
    so = r4(src)[:, :, :, 1:2]
    de = r4(dst)[:, :, :, 0:1]
    do = r4(dst)[:, :, :, 1:2]
    cosb = cos_i.unsqueeze(1).broadcast_to([128, nheads, 32]).unsqueeze(3)
    sinb = sin_i.unsqueeze(1).broadcast_to([128, nheads, 32]).unsqueeze(3)
    A = scrA[:, 0:hw].rearrange("p (h d) -> p h d", d=32).unsqueeze(3)
    B = scrB[:, 0:hw].rearrange("p (h d) -> p h d", d=32).unsqueeze(3)
    e_tt(P, A, se, cosb, ALU.mult)
    e_tt(P, B, so, sinb, ALU.mult)
    e_tt(P, de, A, B, ALU.subtract)
    e_tt(P, A, se, sinb, ALU.mult)
    e_tt(P, B, so, cosb, ALU.mult)
    e_tt(P, do, A, B, ALU.add)


def qk_unitnorm(P, x_ap, nheads, scr, sq_scr):
    w = nheads * 64
    nh = nheads
    e_tt(P, sq_scr[:, 0:w], x_ap, x_ap, ALU.mult)
    u = scr[:, 0:nh]
    e_red(P, u, sq_scr[:, 0:w].rearrange("p (h d) -> p h d", d=64), ALU.add)
    s0 = scr[:, 8:8 + nh]
    r0 = scr[:, 16:16 + nh]
    t1 = scr[:, 24:24 + nh]
    e_sqrt(P, s0, u)
    e_ts(P, s0, s0, 1e-30, None, ALU.max)
    e_recip(P, r0, s0)
    e_tt(P, t1, u, r0, ALU.mult)
    e_tt(P, t1, t1, s0, ALU.add)
    e_ts(P, t1, t1, 0.5, None, ALU.mult)
    e_ts(P, t1, t1, 1e-12, None, ALU.max)
    e_recip(P, r0, t1)
    rcb = r0.unsqueeze(2).broadcast_to([128, nh, 64])
    x3 = x_ap.rearrange("p (h d) -> p h d", d=64)
    e_tt(P, x3, x3, rcb, ALU.mult)


def round_ladder(P, dst, src, scale_ap, f32_scr, m_scr, i32_scr, hr=6):
    e_ts(P, f32_scr, src, scale_ap, 200.5, ALU.mult, ALU.add)
    e_copy(P, "vector", i32_scr, f32_scr)
    e_tt(P, dst, f32_scr, i32_scr, ALU.subtract)
    e_ts(P, m_scr, dst, float(-hr), None, ALU.is_ge)
    for k in range(-hr + 1, hr + 1):
        e_stt(P, m_scr, dst, float(k), m_scr, ALU.is_ge, ALU.add)
    e_stt(P, dst, i32_scr, 200.0 + hr + 1.0, m_scr, ALU.subtract, ALU.add)


def round_halfup(P, dst, src, scale_ap, f32_scr, m_scr, i32_scr):
    e_ts(P, f32_scr, src, scale_ap, 200.5, ALU.mult, ALU.add)
    e_copy(P, "vector", i32_scr, f32_scr)
    e_tt(P, m_scr, i32_scr, f32_scr, ALU.is_gt)
    e_stt(P, dst, i32_scr, 200.0, m_scr, ALU.subtract, ALU.subtract)
    e_stt(P, m_scr, i32_scr, 1.0, f32_scr, ALU.add, ALU.is_le)
    e_tt(P, dst, dst, m_scr, ALU.add)
    e_stt(P, m_scr, dst, 200.0, f32_scr, ALU.add, ALU.is_gt)
    e_tt(P, dst, dst, m_scr, ALU.subtract)
    e_stt(P, m_scr, dst, 201.0, f32_scr, ALU.add, ALU.is_le)
    e_tt(P, dst, dst, m_scr, ALU.add)


def e_rsqrt_newton(P, dst, u, scr, c0):
    s0 = scr[:, c0:c0 + 1]
    r0 = scr[:, c0 + 1:c0 + 2]
    t1 = scr[:, c0 + 2:c0 + 3]
    e_sqrt(P, s0, u)
    e_recip(P, r0, s0)
    e_tt(P, t1, r0, r0, ALU.mult)
    e_tt(P, t1, t1, u, ALU.mult)
    e_ts(P, t1, t1, -0.5, 1.5, ALU.mult, ALU.add)
    e_tt(P, dst, r0, t1, ALU.mult)


def e_sqrt_newton(P, dst, u, scr, c0):
    s0 = scr[:, c0:c0 + 1]
    r0 = scr[:, c0 + 1:c0 + 2]
    t1 = scr[:, c0 + 2:c0 + 3]
    e_sqrt(P, s0, u)
    e_ts(P, s0, s0, 1e-30, None, ALU.max)
    e_recip(P, r0, s0)
    e_tt(P, t1, u, r0, ALU.mult)
    e_tt(P, t1, t1, s0, ALU.add)
    e_ts(P, dst, t1, 0.5, None, ALU.mult)



def build_program(n_stages=None):
    if n_stages is None:
        n_stages = N_STAGES
    nc = bass.Bass("TRN2", target_bir_lowering=False)

    def din(name, shape, dtype=F32):
        return nc.dram_tensor(name, list(shape), dtype, kind="ExternalInput").ap()

    x_in = din("x", (S, D))
    out_t = nc.dram_tensor("out", [S, D], F32, kind="ExternalOutput").ap()

    wqkvT0 = din("wqkvT0", (D, 3 * D))
    wprojT0 = din("wprojT0", (D, D))
    w1T = din("w1T", (D, DFF))
    w2T = din("w2T", (DFF, D))
    b1bc = din("b1bc", (128, DFF))
    b2bc = din("b2bc", (128, D))
    wqkvT1 = din("wqkvT1", (D, 3 * D))
    bqkv1bc = din("bqkv1bc", (128, 3 * D))
    woT1 = din("woT1", (D, D))
    bo1bc = din("bo1bc", (128, D))
    wT_l1f = din("wT_l1f", (D, D))
    bl1f_bc = din("bl1f_bc", (128, D))
    lora_downT = din("lora_downT", (100, 4))
    lora_upT = din("lora_upT", (4, 50))
    down_aux = din("down_aux", (100, D))
    up_auxT = din("up_auxT", (50, D))
    identf = din("identf", (128, 128))
    identb = din("identb", (128, 128), BF16)
    cmask = din("cmask", (128, 128))
    ropeC = din("ropeC", (S, 32))
    ropeS = din("ropeS", (S, 32))
    tgtc = din("tgtc", (128, NT))
    iota8 = din("iota8", (128, 8))
    w1q_d = nc.dram_tensor("w1q_d", [D, DFF], BF16, kind="Internal").ap()
    w2q_d = nc.dram_tensor("w2q_d", [DFF, D], BF16, kind="Internal").ap()

    def tap(name, ap):
        if name not in TAPS:
            return
        t = nc.dram_tensor("tap_" + name, list(ap.shape), ap.dtype,
                           kind="ExternalOutput").ap()
        P.dma("dse", t, ap)

    es = ExitStack()
    with es:
        P = Prog(nc, es)
        sbg = lambda nm, shp, dt=F32: es.enter_context(nc.sbuf_tensor(nm, shp, dt))[:]

        X = []
        IDF = sbg("IDF", [128, 128])
        IDB = sbg("IDB", [128, 128], BF16)
        CMs = sbg("CMs", [128, 128])
        TCs = sbg("TCs", [128, NT])
        IO8 = sbg("IO8", [128, 8])
        GSM = sbg("GSM", [128, 64])

        if n_stages == 0:
            X.extend(sbg(f"X{i}", [128, D]) for i in range(NT))
            for i in range(NT):
                P.dma("dge", X[i], x_in[i * 128:(i + 1) * 128, :])
        P.dma("dsy", IDF, identf)
        P.dma("dsy", IDB, identb)
        P.dma("dsy", CMs, cmask)
        P.dma("dsy", TCs, tgtc)
        P.dma("dsy", IO8, iota8)
        P.flush_block()

        def attn_layer(layer):
            wqkvT = wqkvT0 if layer == 0 else wqkvT1
            wpT = wprojT0 if layer == 0 else woT1
            with ExitStack() as ls:
                sl_ = lambda nm, shp, dt=F32: ls.enter_context(
                    nc.sbuf_tensor(uname(nm), shp, dt, side="right"))[:]
                AVT = [sl_(f"AVT{c}", [128, S]) for c in range(8)]
                for hg in range(NHG):
                    with ExitStack() as hs:
                        s2 = lambda nm, shp, dt=F32: hs.enter_context(
                            nc.sbuf_tensor(uname(nm), shp, dt))[:]
                        p2 = lambda nm, shp: hs.enter_context(
                            nc.psum_tensor(uname(nm), shp, F32))[:]
                        WQ = [s2(f"WQ{p}", [128, 512]) for p in range(2)]
                        KT = [s2(f"KT{c}", [128, S]) for c in range(2)]
                        VS = [s2(f"VS{i}", [128, 256]) for i in range(NT)]
                        BQ = [s2(f"BQ{p}", [128, 256]) for p in range(2)] \
                            if layer == 1 else None
                        XTt = [[s2(f"XTt{t}_{f}", [128, 128]) for f in range(8)]
                               for t in range(2)]
                        QTt = [[s2(f"QTt{t}_{c}", [128, 128]) for c in range(2)]
                               for t in range(2)]
                        RW1 = s2("RW1", [128, 512])
                        QRr = s2("QRr", [128, 256])
                        RA = s2("RA", [128, 128])
                        RB = s2("RB", [128, 128])
                        RCB = [s2(f"RCB{t}", [128, 32]) for t in range(2)]
                        RSB = [s2(f"RSB{t}", [128, 32]) for t in range(2)]
                        if layer == 0:
                            SQs = s2("SQs", [128, 256])
                            NSM = s2("NSM", [128, 64])
                        nS = 4 if layer == 0 else 2
                        nP = 4 if layer == 0 else 1
                        Ssb = [s2(f"Ssb{p}", [128, S]) for p in range(nS)]
                        Psb = [s2(f"Psb{p}", [128, S]) for p in range(nP)]
                        PTT = [s2(f"PTT{p}", [128, 256 if layer == 0 else 128])
                               for p in range(2)]
                        STT = s2("STT", [128, 64])
                        if layer == 0:
                            J16 = s2("J16", [128, S], BF16)
                            JS16 = s2("JS16", [128, S], BF16)
                            M8 = s2("M8", [128, 64])
                            OH = s2("OH", [128, 64])
                        PPART = p2("PPART", [128, 1024])
                        PSs = [p2(f"PSs{p}", [128, 512]) for p in range(2)]
                        PTp = [p2(f"PTp{p}", [128, 128]) for p in range(2)]
                        PAV = [p2(f"PAV{p}", [64, 256]) for p in range(2)]

                        XIN = [s2(f"XIN{t}", [128, D]) for t in range(2)] \
                            if layer == 0 else None

                        def sbuf_idx(t, hl):
                            if layer == 0:
                                return t * 2 + (hl % 2)
                            return t

                        PENDAV = [None]
                        for ib in range(NT // 2):
                            tt2 = [ib * 2, ib * 2 + 1]
                            for t, i in enumerate(tt2):
                                P.dma("dge", RCB[t], ropeC[i * 128:(i + 1) * 128, :])
                                P.dma("dge", RSB[t], ropeS[i * 128:(i + 1) * 128, :])
                                if layer == 0:
                                    P.dma("dge", XIN[t],
                                          x_in[i * 128:(i + 1) * 128, :])
                                xi = XIN[t] if layer == 0 else X[i]
                                for f in range(8):
                                    e_trans(P, PTp[f % 2],
                                            xi[:, f * 128:(f + 1) * 128], IDF)
                                    e_copy(P, "scalar", XTt[t][f], PTp[f % 2])
                            c0q = hg * 256
                            c0k = D + hg * 256
                            for f in range(8):
                                wb = WQ[f % 2]
                                P.dma("dsy", wb[:, 0:256],
                                      wqkvT[f * 128:(f + 1) * 128, c0q:c0q + 256])
                                P.dma("dsy", wb[:, 256:512],
                                      wqkvT[f * 128:(f + 1) * 128, c0k:c0k + 256])
                                for t in range(2):
                                    e_mm(P, PPART[:, t * 512:(t + 1) * 512],
                                         XTt[t][f], wb, start=(f == 0), stop=(f == 7))
                            for t, i in enumerate(tt2):
                                for part in range(2):
                                    sl = PPART[:, t * 512 + part * 256:
                                               t * 512 + part * 256 + 256]
                                    e_copy(P, "scalar", RW1[:, part * 256:
                                                            part * 256 + 256], sl)
                                if layer == 1:
                                    P.dma("dsy", BQ[0], bqkv1bc[:, c0q:c0q + 256])
                                    P.dma("dsy", BQ[1], bqkv1bc[:, c0k:c0k + 256])
                                    e_tt(P, RW1[:, 0:256], RW1[:, 0:256],
                                         BQ[0], ALU.add)
                                    e_tt(P, RW1[:, 256:512], RW1[:, 256:512],
                                         BQ[1], ALU.add)
                                for part in range(2):
                                    rw = RW1[:, part * 256:part * 256 + 256]
                                    rope_apply(P, QRr, rw, RCB[t], RSB[t],
                                               HPG, RA, RB)
                                    if layer == 0:
                                        qk_unitnorm(P, QRr, HPG, NSM, SQs)
                                    for c in range(2):
                                        e_trans(P, PTp[c],
                                                QRr[:, c * 128:(c + 1) * 128], IDF)
                                        if part == 0:
                                            e_copy(P, "scalar", QTt[t][c], PTp[c])
                                        else:
                                            e_copy(P, "scalar",
                                                   KT[c][:, i * 128:(i + 1) * 128],
                                                   PTp[c])
                            c0v = 2 * D + hg * 256
                            for f in range(8):
                                wb = WQ[f % 2]
                                P.dma("dsy", wb[:, 0:256],
                                      wqkvT[f * 128:(f + 1) * 128, c0v:c0v + 256])
                                for t in range(2):
                                    e_mm(P, PPART[:, t * 512:t * 512 + 256],
                                         XTt[t][f], wb[:, 0:256],
                                         start=(f == 0), stop=(f == 7))
                            for t, i in enumerate(tt2):
                                e_copy(P, "scalar", VS[i],
                                       PPART[:, t * 512:t * 512 + 256])
                                if layer == 1:
                                    P.dma("dsy", BQ[t], bqkv1bc[:, c0v:c0v + 256])
                                    e_tt(P, VS[i], VS[i], BQ[t], ALU.add)

                            def mk_scores(t, i, hl):
                                si = sbuf_idx(t, hl)
                                K_i = (i + 1) * 128
                                ch, ro = hl // 2, 64 * (hl % 2)
                                qT = QTt[t][ch][ro:ro + 64, :]
                                nfull = (i * 128) // 512
                                blocks = [(b * 512, 512) for b in range(nfull)]
                                rem = (i * 128) % 512
                                if rem:
                                    blocks.append((nfull * 512, rem))
                                blocks.append((i * 128, 128))
                                for bi_, (bc0, bw) in enumerate(blocks):
                                    pb = PSs[bi_ % 2]
                                    e_mm(P, pb[:, 0:bw], qT,
                                         KT[ch][ro:ro + 64, bc0:bc0 + bw],
                                         start=True, stop=True)
                                    e_mul(P, Ssb[si][:, bc0:bc0 + bw],
                                          pb[:, 0:bw], 0.125)
                                e_tt(P, Ssb[si][:, i * 128:K_i],
                                     Ssb[si][:, i * 128:K_i], CMs, ALU.add)

                            def mk_softmax_av(t, i, hl, pslot):
                                si = sbuf_idx(t, hl)
                                K_i = (i + 1) * 128
                                col = t + 2 * (hl % 2)
                                z = STT[:, 24 + col:25 + col]
                                rz = STT[:, 28 + col:29 + col]
                                sA = Ssb[si][:, 0:K_i]
                                pb = Psb[pslot][:, 0:K_i]
                                if layer == 0:
                                    hi = STT[:, 16 + col:17 + col]
                                    chi = STT[:, 20 + col:21 + col]
                                    e_ts(P, J16[:, 0:K_i], sA, hi, None,
                                         ALU.is_gt, ALU.add, accum=chi)
                                    e_ts(P, J16[:, 0:K_i], sA, hi, -1000.0,
                                         ALU.is_gt, ALU.mult)
                                    e_tt(P, pb, J16[:, 0:K_i], sA, ALU.add)
                                    e_max8(P, M8[:, col * 8:(col + 1) * 8], pb)
                                    idx = STT[:, 32 + col:33 + col]
                                    e_tt(P, idx, STT[:, 12 + t:13 + t], chi,
                                         ALU.subtract)
                                    e_ts(P, idx, idx, 1.0, 0.0,
                                         ALU.subtract, ALU.max)
                                    e_ts(P, idx, idx, 7.0, None, ALU.min)
                                    e_tt(P, OH[:, col * 8:(col + 1) * 8], IO8,
                                         idx.broadcast_to([128, 8]), ALU.is_equal)
                                    e_tt(P, OH[:, col * 8:(col + 1) * 8],
                                         OH[:, col * 8:(col + 1) * 8],
                                         M8[:, col * 8:(col + 1) * 8], ALU.mult)
                                    tau = STT[:, 36 + col:37 + col]
                                    e_red(P, tau, OH[:, col * 8:(col + 1) * 8],
                                          ALU.add)
                                    e_act(P, pb, sA, AFT.Exp)
                                    e_stt(P, pb, sA, tau, pb,
                                          ALU.is_ge, ALU.mult, accum=z)
                                    e_recip(P, rz, z)
                                    e_act(P, pb, pb, AFT.Copy, scale=rz)
                                    if hg == 0 and i == 1:
                                        tap(f"selstt{hl}", STT[:, 0:20])
                                        tap(f"selchi{hl}", STT[:, 20 + col:21 + col])
                                        tap(f"seltau{hl}", STT[:, 36 + col:37 + col])
                                        tap(f"selidx{hl}", STT[:, 32 + col:33 + col])
                                        tap(f"selm8{hl}", M8[:, col * 8:(col + 1) * 8])
                                        tap(f"selp{hl}", Psb[pslot][:, 0:256])
                                        tap(f"selsA{hl}", Ssb[si][:, 0:256])
                                else:
                                    e_act(P, pb, sA, AFT.Exp, accum=z)
                                    e_recip(P, rz, z)
                                    e_act(P, pb, pb, AFT.Copy, scale=rz)

                            def mk_av_pair(hl, i0, i1):
                                pav = PAV[hl % 2]
                                pb0 = Psb[hl % 2]
                                pb1 = Psb[2 + (hl % 2)] if len(Psb) > 2 else Psb[1]
                                for c in range(i0 + 1):
                                    par = c % 2
                                    e_trans(P, PTp[par],
                                            pb0[:, c * 128:(c + 1) * 128], IDF)
                                    e_copy(P, "scalar", PTT[par][:, 0:128], PTp[par])
                                    e_trans(P, PTp[par],
                                            pb1[:, c * 128:(c + 1) * 128], IDF)
                                    e_copy(P, "scalar", PTT[par][:, 128:256], PTp[par])
                                    e_mm(P, pav, VS[c][:, hl * 64:(hl + 1) * 64],
                                         PTT[par], start=(c == 0), stop=False)
                                par = (i0 + 1) % 2
                                e_trans(P, PTp[par],
                                        pb1[:, i1 * 128:(i1 + 1) * 128], IDF)
                                e_copy(P, "scalar", PTT[par][:, 128:256], PTp[par])
                                e_mm(P, pav[:, 128:256],
                                     VS[i1][:, hl * 64:(hl + 1) * 64],
                                     PTT[par][:, 128:256], start=False, stop=True)
                                e_copy(P, "scalar",
                                       AVT[hg * 2 + hl // 2]
                                       [(hl % 2) * 64:(hl % 2) * 64 + 64,
                                        i0 * 128:i0 * 128 + 256], pav)

                            def mk_select(pr):
                                he, ho = 2 * pr, 2 * pr + 1
                                tv = STT[:, 0:4]
                                cn = STT[:, 4:8]
                                ge = STT[:, 8:12]
                                tg = STT[:, 12:16]
                                hi4 = STT[:, 16:20]
                                for t, i in enumerate(tt2):
                                    e_copy(P, "vector", tg[:, t:t + 1],
                                           TCs[:, i:i + 1])
                                    e_ts(P, tg[:, 2 + t:3 + t], TCs[:, i:i + 1],
                                         2.0, -float((i + 1) * 128),
                                         ALU.mult, ALU.add)
                                e_memset(P, tv, 0.0)
                                ntv = STT[:, 40:42]
                                delta = SEL_DELTA0
                                for r in range(R_SEL):
                                    delta *= 0.5
                                    e_ts(P, ntv, tv[:, 2:4], -1.0, None, ALU.mult)
                                    for t, i in enumerate(tt2):
                                        K_i = (i + 1) * 128
                                        sie = sbuf_idx(t, he)
                                        sio = sbuf_idx(t, ho)
                                        e_ts(P, J16[:, 0:K_i],
                                             Ssb[sie][:, 0:K_i],
                                             tv[:, t:t + 1], None,
                                             ALU.is_gt, ALU.add,
                                             accum=cn[:, t:t + 1])
                                        def em_sign(E, d=JS16[:, 0:K_i],
                                                    s=Ssb[sio][:, 0:K_i],
                                                    b=ntv[:, t:t + 1],
                                                    ac=cn[:, 2 + t:3 + t]):
                                            return E.activation(
                                                out=d, in_=s, func=AFT.Sign,
                                                bias=b, scale=1.0,
                                                accum_out=ac)
                                        P.op("scalar", em_sign,
                                             reads=[Ssb[sio], STT],
                                             writes=[JS16, STT])
                                    e_tt(P, ge, cn, tg, ALU.is_ge)
                                    e_stt(P, tv, ge, 2.0 * delta, tv,
                                          ALU.mult, ALU.add)
                                    e_ts(P, tv, tv, delta, None, ALU.subtract)
                                e_ts(P, hi4, tv, delta, None, ALU.add)

                            i0, i1 = tt2[0], tt2[1]
                            if layer == 0:
                                for t, i in enumerate(tt2):
                                    mk_scores(t, i, 0)
                                    mk_scores(t, i, 1)
                                mk_select(0)
                                if PENDAV[0] is not None:
                                    pi0, pi1 = PENDAV[0]
                                    mk_av_pair(2, pi0, pi1)
                                    mk_av_pair(3, pi0, pi1)
                                    PENDAV[0] = None
                                mk_softmax_av(0, i0, 0, 0)
                                mk_softmax_av(1, i1, 0, 2)
                                mk_softmax_av(0, i0, 1, 1)
                                mk_softmax_av(1, i1, 1, 3)
                                mk_scores(0, i0, 2)
                                mk_scores(1, i1, 2)
                                mk_scores(0, i0, 3)
                                mk_scores(1, i1, 3)
                                mk_select(1)
                                mk_av_pair(0, i0, i1)
                                mk_av_pair(1, i0, i1)
                                mk_softmax_av(0, i0, 2, 0)
                                mk_softmax_av(1, i1, 2, 2)
                                mk_softmax_av(0, i0, 3, 1)
                                mk_softmax_av(1, i1, 3, 3)
                                PENDAV[0] = (i0, i1)
                            else:
                                def mk_av1(t, i, hl):
                                    pb = Psb[0]
                                    for c in range(i + 1):
                                        par = c % 2
                                        e_trans(P, PTp[par],
                                                pb[:, c * 128:(c + 1) * 128], IDF)
                                        e_copy(P, "scalar", PTT[par], PTp[par])
                                        e_mm(P, PAV[hl % 2][:, 0:128],
                                             VS[c][:, hl * 64:(hl + 1) * 64],
                                             PTT[par], start=(c == 0), stop=(c == i))
                                    e_copy(P, "scalar",
                                           AVT[hg * 2 + hl // 2]
                                           [(hl % 2) * 64:(hl % 2) * 64 + 64,
                                            i * 128:(i + 1) * 128],
                                           PAV[hl % 2][:, 0:128])
                                for hl in range(HPG):
                                    for t, i in enumerate(tt2):
                                        mk_scores(t, i, hl)
                                    mk_softmax_av(0, i0, hl, 0)
                                    mk_av1(0, i0, hl)
                                    mk_softmax_av(1, i1, hl, 0)
                                    mk_av1(1, i1, hl)
                        if layer == 0 and PENDAV[0] is not None:
                            pi0, pi1 = PENDAV[0]
                            mk_av_pair(2, pi0, pi1)
                            mk_av_pair(3, pi0, pi1)
                            PENDAV[0] = None
                        P.flush_block()
                for c_ in range(8):
                    tap(f"avt{layer}_{c_}", AVT[c_][:, 128:256])
                if layer == 0 and not X:
                    X.extend(sbg(f"X{i}", [128, D]) for i in range(NT))
                with ExitStack() as pj:
                    s3 = lambda nm, shp, dt=F32: pj.enter_context(
                        nc.sbuf_tensor(uname(nm), shp, dt))[:]
                    WPJ = [s3(f"WPJ{f}", [128, D]) for f in range(8)]
                    BO = s3("BO", [128, D]) if layer == 1 else None
                    PPJ = pj.enter_context(
                        nc.psum_tensor(uname("PPJ"), [128, D], F32))[:]
                    for f in range(8):
                        P.dma("dsy", WPJ[f], wpT[f * 128:(f + 1) * 128, :])
                    if layer == 1:
                        P.dma("dsy", BO, bo1bc)
                    for i in range(NT):
                        if layer == 0:
                            P.dma("dge", X[i], x_in[i * 128:(i + 1) * 128, :])
                        for f in range(8):
                            for o2 in range(2):
                                e_mm(P, PPJ[:, o2 * 512:(o2 + 1) * 512],
                                     AVT[f][:, i * 128:(i + 1) * 128],
                                     WPJ[f][:, o2 * 512:(o2 + 1) * 512],
                                     start=(f == 0), stop=(f == 7))
                        e_tt(P, X[i], X[i], PPJ, ALU.add)
                        if layer == 1:
                            e_tt(P, X[i], X[i], BO, ALU.add)
                        layernorm_inplace(P, X[i], GSM)
                    P.flush_block()

        def l0_ffn():
            with ExitStack() as ph0:
                RSW = ph0.enter_context(nc.sbuf_tensor(uname("RSW"), [128, 2], F32))[:]
                l0_ffn_quant(RSW)
                l0_ffn_main(RSW)

        def l0_ffn_quant(RSW):
            with ExitStack() as qs:
                s2 = lambda nm, shp, dt=F32: qs.enter_context(nc.sbuf_tensor(uname(nm), shp, dt))[:]
                WS = [s2(f"WS{p}", [128, DFF]) for p in range(2)]
                WQB = [s2(f"WQB{p}", [128, DFF], BF16) for p in range(2)]
                WI = s2("WI", [128, 512], I32)
                RSC = s2("RSC", [128, 512])
                RSM = s2("RSM", [128, 512])
                CS = s2("CS", [128, 64])
                ONE = s2("ONE", [128, 1])
                ONER = s2("ONER", [1, 128])
                PS1 = qs.enter_context(nc.psum_tensor(uname("PS1"), [128, 8], F32))[:]
                e_memset(P, ONE, 1.0)
                e_memset(P, ONER, 1.0)
                for (wsrc, wdst, nchunk, wid, csl) in (
                        (w1T, w1q_d, 8, DFF, 0), (w2T, w2q_d, 32, D, 1)):
                    acc = CS[:, 4 + csl:5 + csl]
                    e_memset(P, acc, 0.0)
                    for f in range(nchunk):
                        p = f % 2
                        P.dma("dsy" if p == 0 else "dge",
                              WS[p][:, 0:wid], wsrc[f * 128:(f + 1) * 128, :])
                        nseg = wid // 128
                        e_red(P, CS[:, 20:20 + nseg],
                              WS[p][:, 0:wid].rearrange("p (g k) -> p g k", k=128),
                              ALU.add, absval=True)
                        e_red(P, CS[:, p:p + 1], CS[:, 20:20 + nseg], ALU.add)
                        e_tt(P, acc, acc, CS[:, p:p + 1], ALU.add)
                    e_mm(P, PS1[0:1, 0:1], ONE, acc, start=True, stop=True)
                    e_copy(P, "vector", CS[0:1, 8 + csl:9 + csl], PS1[0:1, 0:1])
                    e_ts(P, CS[0:1, 10 + csl:11 + csl], CS[0:1, 8 + csl:9 + csl],
                         1.0 / (D * DFF), 1e-5, ALU.mult, ALU.max)
                    e_recip(P, CS[0:1, 12 + csl:13 + csl], CS[0:1, 10 + csl:11 + csl])
                    e_mm(P, PS1[:, 1:2], ONER,
                         CS[0:1, 12 + csl:13 + csl], start=True, stop=True)
                    e_copy(P, "vector", CS[:, 14 + csl:15 + csl], PS1[:, 1:2])
                    e_mm(P, PS1[:, 2:3], ONER,
                         CS[0:1, 10 + csl:11 + csl], start=True, stop=True)
                    e_copy(P, "vector", CS[:, 16 + csl:17 + csl], PS1[:, 2:3])
                    sw = CS[:, 14 + csl:15 + csl]
                    for f in range(nchunk):
                        p = f % 2
                        P.dma("dsy" if p == 0 else "dge",
                              WS[p][:, 0:wid], wsrc[f * 128:(f + 1) * 128, :])
                        for sl in range(0, wid, 512):
                            round_halfup(P, WQB[p][:, sl:sl + 512],
                                         WS[p][:, sl:sl + 512], sw, RSC, RSM, WI)
                        e_ts(P, WQB[p][:, 0:wid], WQB[p][:, 0:wid], -1.0, 1.0,
                             ALU.max, ALU.min)
                        P.dma("dse", wdst[f * 128:(f + 1) * 128, :], WQB[p][:, 0:wid])
                e_copy(P, "vector", RSW[:, 0:1], CS[:, 16:17])
                e_copy(P, "vector", RSW[:, 1:2], CS[:, 17:18])
                tap("ffn_cs", CS)
                P.flush_block()
                tap("ffn_w1q", w1q_d[0:128, :])

        def l0_ffn_main(RSW):
                rsw1g = RSW[:, 0:1]
                rsw2g = RSW[:, 1:2]
                with ExitStack() as ms:
                    s3 = lambda nm, shp, dt=F32: ms.enter_context(
                        nc.sbuf_tensor(uname(nm), shp, dt))[:]
                    B1W = [s3(f"B1W{p}", [128, 512]) for p in range(2)]
                    B2s = s3("B2s", [128, D])
                    P.dma("dsy", B2s, b2bc)
                    W1B = [s3(f"W1B{p}", [128, 512], BF16) for p in range(2)]
                    W2B = [s3(f"W2B{p}", [128, 512], BF16) for p in range(2)]
                    XQT = [[s3(f"XQT{t}_{f}", [128, 128], BF16) for f in range(8)]
                           for t in range(4)]
                    HQT = [[s3(f"HQT{t}_{f}", [128, 128], BF16) for f in range(32)]
                           for t in range(4)]
                    HSB = [s3(f"HSB{t}", [128, DFF]) for t in range(4)]
                    XN = s3("XN", [128, D])
                    XQ = s3("XQ", [128, D], BF16)
                    HNS = s3("HNS", [128, 512])
                    GSC = s3("GSC", [128, 512])
                    RSC2 = s3("RSC2", [128, 512])
                    RSM2 = s3("RSM2", [128, 512])
                    HQs = s3("HQs", [128, 512], BF16)
                    WI2 = s3("WI2", [128, 512], I32)
                    QS1 = s3("QS1", [128, 8])
                    QS2 = s3("QS2", [128, 8])
                    SM = s3("SM", [128, 32])
                    SMB = s3("SMB", [128, 64])
                    PH = [ms.enter_context(nc.psum_tensor(uname(f"PH{t}"), [128, 512], F32))[:]
                          for t in range(4)]
                    PTq = [ms.enter_context(nc.psum_tensor(uname(f"PTq{p}"), [128, 128], BF16))[:]
                           for p in range(2)]
                    for tg in range(4):
                        tt4 = [tg * 4 + t for t in range(4)]
                        for t, i in enumerate(tt4):
                            e_tt(P, HSB[t][:, 0:D], X[i], X[i], ALU.mult)
                            e_red(P, SM[:, 0:1], HSB[t][:, 0:D], ALU.add)
                            e_ts(P, SM[:, 1:2], SM[:, 0:1], 1.0 / D, 1e-6,
                                 ALU.mult, ALU.add)
                            e_rsqrt_newton(P, SM[:, 3:4], SM[:, 1:2], SMB, 60)
                            e_ts(P, XN, X[i], SM[:, 3:4], None, ALU.mult)
                            e_red(P, SM[:, 4:5], XN, ALU.max, absval=True)
                            e_ts(P, SM[:, 5:6], SM[:, 4:5], 1e-5, None, ALU.max)
                            e_recip(P, SM[:, 6:7], SM[:, 5:6])
                            e_ts(P, SM[:, 7:8], SM[:, 6:7], 127.0, None, ALU.mult)
                            for sl in range(0, D, 512):
                                round_halfup(P, XQ[:, sl:sl + 512], XN[:, sl:sl + 512],
                                             SM[:, 7:8], RSC2, RSM2, WI2)
                            e_ts(P, SM[:, 8:9], SM[:, 5:6], 1.0 / 127.0, None, ALU.mult)
                            e_tt(P, QS1[:, t:t + 1], SM[:, 8:9], rsw1g, ALU.mult)
                            if i == 0:
                                tap("ffn_xq", XQ)
                                tap("ffn_xn", XN)
                                pass
                            for f in range(8):
                                e_trans(P, PTq[f % 2], XQ[:, f * 128:(f + 1) * 128], IDB)
                                e_copy(P, "scalar", XQT[t][f], PTq[f % 2])
                        for ow in range(8):
                            for f in range(8):
                                wb = W1B[f % 2]
                                P.dma("dsy" if f % 2 == 0 else "dge",
                                      wb, w1q_d[f * 128:(f + 1) * 128,
                                                ow * 512:(ow + 1) * 512])
                                for t in range(4):
                                    e_mm(P, PH[t], XQT[t][f], wb,
                                         start=(f == 0), stop=(f == 7))
                            P.dma("dse", B1W[ow % 2], b1bc[:, ow * 512:(ow + 1) * 512])
                            for t in range(4):
                                hsl = HSB[t][:, ow * 512:(ow + 1) * 512]
                                e_stt(P, hsl, PH[t],
                                      QS1[:, t:t + 1], B1W[ow % 2], ALU.mult, ALU.add)
                                e_act(P, hsl, hsl, AFT.Gelu)
                        if tg == 0:
                            tap("ffn_h0", HSB[0])
                            tap("ffn_qs1", QS1[:, 0:4])
                        for t, i in enumerate(tt4):
                            bn_meanvar(P, HSB[t], SMB, 50, 51)
                            e_tt(P, SM[:, 10:11], SMB[:, 50:51], SMB[:, 50:51], ALU.mult)
                            e_tt(P, SM[:, 11:12], SMB[:, 51:52], SM[:, 10:11], ALU.add)
                            e_ts(P, SM[:, 12:13], SM[:, 11:12], 1e-6, None, ALU.add)
                            e_rsqrt_newton(P, SM[:, 14:15], SM[:, 12:13], SMB, 60)
                            for sl in range(8):
                                e_ts(P, HNS, HSB[t][:, sl * 512:(sl + 1) * 512],
                                     SM[:, 14:15], None, ALU.mult)
                                e_red(P, SM[:, 16 + sl:17 + sl], HNS, ALU.max, absval=True)
                            e_red(P, SM[:, 24:25], SM[:, 16:24], ALU.max)
                            e_ts(P, SM[:, 25:26], SM[:, 24:25], 1e-5, None, ALU.max)
                            e_recip(P, SM[:, 26:27], SM[:, 25:26])
                            e_ts(P, SM[:, 27:28], SM[:, 26:27], 127.0, None, ALU.mult)
                            e_ts(P, SM[:, 28:29], SM[:, 25:26], 1.0 / 127.0, None, ALU.mult)
                            e_tt(P, QS2[:, t:t + 1], SM[:, 28:29], rsw2g, ALU.mult)
                            for sl in range(8):
                                e_ts(P, HNS, HSB[t][:, sl * 512:(sl + 1) * 512],
                                     SM[:, 14:15], None, ALU.mult)
                                round_ladder(P, HQs, HNS, SM[:, 27:28], RSC2, RSM2, WI2)
                                if tg == 0 and t == 0 and sl == 0:
                                    tap("ffn_hq0", HQs)
                                    tap("ffn_hn0", HNS)
                                    tap("ffn_s2", SM[:, 27:28])
                                for f2 in range(4):
                                    e_trans(P, PTq[f2 % 2], HQs[:, f2 * 128:(f2 + 1) * 128],
                                            IDB)
                                    e_copy(P, "scalar", HQT[t][sl * 4 + f2], PTq[f2 % 2])
                        for ow in range(2):
                            for f2 in range(32):
                                wb = W2B[f2 % 2]
                                P.dma("dsy" if f2 % 2 == 0 else "dge",
                                      wb, w2q_d[f2 * 128:(f2 + 1) * 128,
                                                ow * 512:(ow + 1) * 512])
                                for t in range(4):
                                    e_mm(P, PH[t], HQT[t][f2], wb,
                                         start=(f2 == 0), stop=(f2 == 31))
                            for t, i in enumerate(tt4):
                                e_stt(P, HSB[t][:, ow * 512:(ow + 1) * 512], PH[t],
                                      QS2[:, t:t + 1], B2s[:, ow * 512:(ow + 1) * 512],
                                      ALU.mult, ALU.add)
                        for t, i in enumerate(tt4):
                            e_tt(P, X[i], X[i], HSB[t][:, 0:D], ALU.add)
                            layernorm_inplace(P, X[i], GSM)
                    P.flush_block()

        def l1_ffn():
            with ExitStack() as ph:
                s2 = lambda nm, shp, dt=F32: ph.enter_context(nc.sbuf_tensor(uname(nm), shp, dt))[:]
                LDT = s2("LDT", [100, 4])
                LUT = s2("LUT", [4, 50])
                DAX = s2("DAX", [100, D])
                UAT = s2("UAT", [50, D])
                T1 = s2("T1", [4, D])
                T2 = s2("T2", [50, D])
                T3 = s2("T3", [128, D])
                WTOT = [s2(f"WTOT{f}", [128, D]) for f in range(8)]
                BL = s2("BL", [128, D])
                XTf = [s2(f"XTf{p}", [128, 128]) for p in range(2)]
                FO = s2("FO", [128, D])
                PL = ph.enter_context(nc.psum_tensor(uname("PL"), [128, D], F32))[:]
                PT2 = [ph.enter_context(nc.psum_tensor(uname(f"PT2{p}"), [128, 128], F32))[:]
                       for p in range(2)]
                P.dma("dsy", LDT, lora_downT)
                P.dma("dsy", LUT, lora_upT)
                P.dma("dsy", DAX, down_aux)
                P.dma("dsy", UAT, up_auxT)
                P.dma("dsy", BL, bl1f_bc)
                for f in range(8):
                    P.dma("dsy", WTOT[f], wT_l1f[f * 128:(f + 1) * 128, :])
                for o2 in range(2):
                    e_mm(P, PL[0:4, o2 * 512:(o2 + 1) * 512], LDT,
                         DAX[:, o2 * 512:(o2 + 1) * 512], start=True, stop=True)
                e_copy(P, "vector", T1, PL[0:4, :])
                for o2 in range(2):
                    e_mm(P, PL[0:50, o2 * 512:(o2 + 1) * 512], LUT,
                         T1[:, o2 * 512:(o2 + 1) * 512], start=True, stop=True)
                e_copy(P, "vector", T2, PL[0:50, :])
                for oc in range(8):
                    for o2 in range(2):
                        e_mm(P, PL[:, o2 * 512:(o2 + 1) * 512],
                             UAT[:, oc * 128:(oc + 1) * 128],
                             T2[:, o2 * 512:(o2 + 1) * 512], start=True, stop=True)
                    e_copy(P, "vector", T3, PL)
                    for f in range(8):
                        e_trans(P, PT2[f % 2], T3[:, f * 128:(f + 1) * 128], IDF)
                        e_tt(P, WTOT[f][:, oc * 128:(oc + 1) * 128],
                             WTOT[f][:, oc * 128:(oc + 1) * 128], PT2[f % 2], ALU.add)
                for i in range(NT):
                    for f in range(8):
                        e_trans(P, PT2[f % 2], X[i][:, f * 128:(f + 1) * 128], IDF)
                        e_copy(P, "scalar", XTf[f % 2], PT2[f % 2])
                        for o2 in range(2):
                            e_mm(P, PL[:, o2 * 512:(o2 + 1) * 512], XTf[f % 2],
                                 WTOT[f][:, o2 * 512:(o2 + 1) * 512],
                                 start=(f == 0), stop=(f == 7))
                    e_tt(P, FO, PL, BL, ALU.add)
                    e_tt(P, X[i], X[i], FO, ALU.add)
                    layernorm_inplace(P, X[i], GSM)
                P.flush_block()

        if n_stages >= 1:
            attn_layer(0)
        if n_stages >= 2:
            l0_ffn()
        if n_stages >= 3:
            attn_layer(1)
        if n_stages >= 4:
            l1_ffn()

        for i in range(NT):
            P.dma("dge", out_t[i * 128:(i + 1) * 128, :], X[i])
        P.wait_all_dma("gpsimd")
        P.flush_block()

    return nc



def _host_consts():
    c = {}
    c["identf"] = np.eye(128, dtype=np.float32)
    c["identb"] = np.eye(128, dtype=np.float32).astype(ml_dtypes.bfloat16)
    m = np.zeros((128, 128), np.float32)
    m[np.triu_indices(128, 1)] = NEG
    c["cmask"] = m
    import jax
    import jax.numpy as jnp
    with jax.default_device(jax.devices("cpu")[0]):
        inv = 1.0 / 10000.0 ** (jnp.arange(0, HD, 2, dtype=jnp.float32) / HD)
        tpos = jnp.arange(S, dtype=jnp.float32)
        fr = jnp.einsum("i,j->ij", tpos, inv)
        c["ropeC"] = np.asarray(jnp.cos(fr), dtype=np.float32)
        c["ropeS"] = np.asarray(jnp.sin(fr), dtype=np.float32)
    q = (np.arange(NT)[None, :] * 128 + np.arange(128)[:, None]).astype(np.float64)
    c["tgtc"] = np.minimum(q + 1.0, 128.0).astype(np.float32)
    c["iota8"] = np.broadcast_to(np.arange(8, dtype=np.float32), (128, 8)).copy()
    return c


def _host_tensors(inputs):
    T = {}
    f = np.float32
    cc = np.ascontiguousarray
    T["wqkvT0"] = cc(np.asarray(inputs["L0_attn_w_qkv"]).T.astype(f))
    T["wprojT0"] = cc(np.asarray(inputs["L0_attn_w_proj"]).T.astype(f))
    T["w1T"] = cc(np.asarray(inputs["L0_ffn_w1"]).T.astype(f))
    T["w2T"] = cc(np.asarray(inputs["L0_ffn_w2"]).T.astype(f))
    T["b1bc"] = cc(np.broadcast_to(np.asarray(inputs["L0_ffn_b1"]).astype(f), (128, DFF)))
    T["b2bc"] = cc(np.broadcast_to(np.asarray(inputs["L0_ffn_b2"]).astype(f), (128, D)))
    T["wqkvT1"] = cc(np.concatenate(
        [np.asarray(inputs["L1_attn_Wq"]).T, np.asarray(inputs["L1_attn_Wk"]).T,
         np.asarray(inputs["L1_attn_Wv"]).T], axis=1).astype(f))
    T["bqkv1bc"] = cc(np.broadcast_to(np.concatenate(
        [np.asarray(inputs["L1_attn_bq"]), np.asarray(inputs["L1_attn_bk"]),
         np.asarray(inputs["L1_attn_bv"])]).astype(f), (128, 3 * D)))
    T["woT1"] = cc(np.asarray(inputs["L1_attn_Wo"]).T.astype(f))
    T["bo1bc"] = cc(np.broadcast_to(np.asarray(inputs["L1_attn_bo"]).astype(f), (128, D)))
    T["wT_l1f"] = cc(np.asarray(inputs["L1_ffn_W"]).T.astype(f))
    T["bl1f_bc"] = cc(np.broadcast_to(np.asarray(inputs["L1_ffn_b"]).astype(f), (128, D)))
    T["lora_downT"] = cc(np.asarray(inputs["L1_ffn_lora_down"]).T.astype(f))
    T["lora_upT"] = cc(np.asarray(inputs["L1_ffn_lora_up"]).T.astype(f))
    T["down_aux"] = cc(np.asarray(inputs["L1_ffn_down_aux"]).astype(f))
    T["up_auxT"] = cc(np.asarray(inputs["L1_ffn_up_aux"]).T.astype(f))
    T.update(_host_consts())
    return T


def kernel(**inputs):
    x = np.asarray(inputs["L0_attn_x"], dtype=np.float32)
    T = _host_tensors(inputs)
    nc = build_program(N_STAGES)
    in_maps = []
    for c in range(N_CORES):
        m = dict(T)
        m["x"] = np.ascontiguousarray(x[c])
        in_maps.append(m)
    res = run_bass_kernel_spmd(nc, in_maps, core_ids=list(range(N_CORES)))
    return np.stack([r["out"] for r in res.results], axis=0)


# version 10
# speedup vs baseline: 1.0749x; 1.0749x over previous
import numpy as np
from contextlib import ExitStack
import concourse.bass as bass
import concourse.mybir as mybir
from concourse.bass_utils import run_bass_kernel_spmd
import ml_dtypes

F32 = mybir.dt.float32
BF16 = mybir.dt.bfloat16
I32 = mybir.dt.int32
ALU = mybir.AluOpType
AFT = mybir.ActivationFunctionType
AXX = mybir.AxisListType.X
AP = bass.AP

N_CORES = 8
S, D = 2048, 1024
NT = S // 128
NH, HD = 16, 64
HPG = 4
NHG = NH // HPG
DFF = 4096
NEG = -1.0e30
R_SEL = 13
SEL_DELTA0 = 0.13

N_STAGES = 4
TAPS = []

ENGS = ("tensor", "vector", "scalar", "gpsimd", "sync")
DMAQ = {"dsy": "sync", "dge": "gpsimd", "dse": "scalar"}


_UNIQ = [0]


def uname(nm):
    _UNIQ[0] += 1
    return f"{nm}_u{_UNIQ[0]}"


def bname(x):
    return x if isinstance(x, str) else x.tensor.name


def nc_sem(nc, name):
    return nc.semaphore(name)


class Prog:
    def __init__(self, nc, es):
        self.nc = nc
        self.es = es
        self.semh = {}
        for e in ENGS:
            self.semh[e] = es.enter_context(nc.semaphore(f"sp_{e}"))
        self.count = {e: 0 for e in ENGS}
        self.dcount = {}
        self.ops = {e: [] for e in ENGS}
        self.buf = {}
        self.watermark = {e: {} for e in ENGS}
        self.covered = {}

    def _dma_role(self, name):
        base = name.split("_u")[0]
        role = "dma_" + base
        if role not in self.semh:
            self.semh[role] = self.es.enter_context(nc_sem(self.nc, role))
            self.dcount[role] = 0
        return role

    def _b(self, name):
        return self.buf.setdefault(name, dict(w=None, r=[]))

    def _edges(self, reads, writes, waw):
        ed = []
        for nm in reads:
            b = self._b(nm)
            if b["w"] is not None:
                ed.append(b["w"])
        for nm in writes:
            b = self._b(nm)
            if waw and b["w"] is not None:
                ed.append(b["w"])
            ed.extend(b["r"])
        return ed

    def _filter(self, eng, ed):
        m = {}
        for s, v in ed:
            if m.get(s, -1) < v:
                m[s] = v
        w = self.watermark[eng]
        out = []
        for s, v in sorted(m.items()):
            if self.covered.get(s, 0) >= v or w.get(s, 0) >= v:
                continue
            out.append((s, v))
            w[s] = v
        return out

    def _update(self, reads, writes, ident):
        for nm in writes:
            b = self._b(nm)
            b["w"] = ident
            b["r"] = []
        for nm in reads:
            self._b(nm)["r"].append(ident)

    def op(self, eng, emit, reads=(), writes=(), waw=True):
        reads = [bname(a) for a in reads]
        writes = [bname(a) for a in writes]
        waits = self._filter(eng, self._edges(reads, writes, waw))
        self.count[eng] += 1
        ident = (eng, self.count[eng])
        self.ops[eng].append((waits, emit, (eng, 1)))
        self._update(reads, writes, ident)
        return ident

    def dma(self, q, out_ap, in_ap):
        eng = DMAQ[q]
        role = self._dma_role(bname(out_ap))
        reads = [bname(in_ap)]
        writes = [bname(out_ap)]
        waits = self._filter(eng, self._edges(reads, writes, True))
        self.dcount[role] += 1
        ident = (role, 16 * self.dcount[role])
        self.ops[eng].append(
            (waits, (lambda E, o=out_ap, i=in_ap: E.dma_start(out=o, in_=i)),
             (role, 16)))
        self._update(reads, writes, ident)
        return ident

    def wait_all_dma(self, eng):
        ed = [(r, 16 * n) for r, n in self.dcount.items() if n > 0]
        waits = self._filter(eng, ed)
        if waits:
            self.ops[eng].append((waits, None, None))

    def flush_block(self):
        nc = self.nc
        if not any(self.ops[e] for e in ENGS):
            return
        if not hasattr(self, "_bar_counts"):
            self._bar_counts = {}
        targets = []
        for e in ENGS:
            if self.count[e] > self._bar_counts.get(e, 0):
                targets.append((e, self.count[e]))
        for r, n in self.dcount.items():
            if 16 * n > self._bar_counts.get(r, 0):
                targets.append((r, 16 * n))
        for e in ENGS:
            waits = [(s, v) for (s, v) in targets]
            if waits:
                self.ops[e].append((waits, None, None))
        for s, v in targets:
            self._bar_counts[s] = v
        with nc.Block() as blk:
            for eng in ENGS:
                if not self.ops[eng]:
                    continue

                def mk(lst):
                    def emit_all(E):
                        for waits, emitf, inc in lst:
                            for (s, v) in waits:
                                E.wait_ge(self.semh[s], v)
                            if emitf is not None:
                                ins = emitf(E)
                                if inc is not None:
                                    ins.then_inc(self.semh[inc[0]], inc[1])
                    return emit_all
                getattr(blk, eng)(mk(self.ops[eng]))
        self.ops = {e: [] for e in ENGS}
        for e in ENGS:
            self.covered[e] = self.count[e]
        for r, n in self.dcount.items():
            self.covered[r] = 16 * n
        self.watermark = {e: {} for e in ENGS}



def e_copy(P, eng, dst, src):
    if eng == "scalar":
        return P.op(eng, lambda E, d=dst, s=src: E.copy(out=d, in_=s),
                    reads=[src], writes=[dst])
    return P.op(eng, lambda E, d=dst, s=src: E.tensor_copy(out=d, in_=s),
                reads=[src], writes=[dst])


def e_mul(P, dst, src, c):
    return P.op("scalar", lambda E, d=dst, s=src, m=c: E.mul(out=d, in_=s, mul=m),
                reads=[src], writes=[dst])


def e_mm(P, psum, lhsT, rhs, start, stop):
    return P.op("tensor",
                lambda E, o=psum, l=lhsT, r=rhs, st=start, sp=stop:
                E.matmul(o, l, r, start=st, stop=sp),
                reads=[lhsT, rhs], writes=[psum], waw=start)


def e_trans(P, psum, src, ident):
    return P.op("tensor", lambda E, o=psum, s=src, i=ident: E.transpose(o, s, i),
                reads=[src, ident], writes=[psum])


def e_ts(P, dst, src, s1, s2, op0, op1=None, accum=None, eng="vector"):
    def em(E, d=dst, s=src, a=s1, b=s2, o0=op0, o1=op1, ac=accum):
        kw = {}
        if o1 is not None:
            kw["op1"] = o1
        if ac is not None:
            kw["accum_out"] = ac
        return E.tensor_scalar(out=d, in0=s, scalar1=a, scalar2=b, op0=o0, **kw)
    reads = [src] + [x for x in (s1, s2) if isinstance(x, AP)]
    writes = [dst] + ([accum] if accum is not None else [])
    return P.op(eng, em, reads=reads, writes=writes)


def e_tt(P, dst, a, b, op, eng="vector"):
    return P.op(eng, lambda E, d=dst, x=a, y=b, o=op:
                E.tensor_tensor(out=d, in0=x, in1=y, op=o),
                reads=[a, b], writes=[dst])


def e_stt(P, dst, in0, scalar, in1, op0, op1, accum=None):
    def em(E, d=dst, i0=in0, sc=scalar, i1=in1, o0=op0, o1=op1, ac=accum):
        kw = {"accum_out": ac} if ac is not None else {}
        return E.scalar_tensor_tensor(out=d, in0=i0, scalar=sc, in1=i1,
                                      op0=o0, op1=o1, **kw)
    reads = [in0, in1] + ([scalar] if isinstance(scalar, AP) else [])
    writes = [dst] + ([accum] if accum is not None else [])
    return P.op("vector", em, reads=reads, writes=writes)


def e_red(P, dst, src, op, absval=False):
    def em(E, d=dst, s=src, o=op, av=absval):
        kw = {"apply_absolute_value": True} if av else {}
        return E.tensor_reduce(out=d, in_=s, axis=AXX, op=o, **kw)
    return P.op("vector", em, reads=[src], writes=[dst])


def e_act(P, dst, src, func, bias=None, scale=None, accum=None):
    def em(E, d=dst, s=src, f=func, b=bias, sc=scale, ac=accum):
        kw = {}
        if b is not None:
            kw["bias"] = b
        if sc is not None:
            kw["scale"] = sc
        if ac is not None:
            kw["accum_out"] = ac
        return E.activation(out=d, in_=s, func=f, **kw)
    reads = [src] + [x for x in (bias, scale) if isinstance(x, AP)]
    writes = [dst] + ([accum] if accum is not None else [])
    return P.op("scalar", em, reads=reads, writes=writes)


def e_max8(P, dst, src):
    return P.op("vector", lambda E, d=dst, s=src: E.max(out=d, in_=s),
                reads=[src], writes=[dst])


def e_recip(P, dst, src):
    return P.op("vector", lambda E, d=dst, s=src: E.reciprocal(out=d, in_=s),
                reads=[src], writes=[dst])


def e_sqrt(P, dst, src):
    return P.op("scalar", lambda E, d=dst, s=src: E.sqrt(out=d, in_=s),
                reads=[src], writes=[dst])


def e_memset(P, dst, val):
    return P.op("vector", lambda E, d=dst, v=val: E.memset(d, v),
                reads=[], writes=[dst])



def bn_meanvar(P, x_ap, scr, mcol, vcol):
    w = x_ap.shape[-1]
    k = (w + 511) // 512
    for j in range(k):
        P.op("vector", lambda E, d=scr[:, 6 * j:6 * j + 6],
             s=x_ap[:, j * 512:min((j + 1) * 512, w)]: E.bn_stats(out=d, in_=s),
             reads=[x_ap], writes=[scr])
    P.op("vector", lambda E, d=scr[:, mcol:vcol + 1], s=scr[:, 0:6 * k]:
         E.bn_aggr(out=d, in_=s), reads=[scr], writes=[scr])


def layernorm_inplace(P, x_ap, scr):
    bn_meanvar(P, x_ap, scr, 50, 51)
    e_ts(P, scr[:, 52:53], scr[:, 51:52], 1e-5, None, ALU.add)
    e_rsqrt_newton(P, scr[:, 56:57], scr[:, 52:53], scr, 57)
    e_ts(P, x_ap, x_ap, scr[:, 50:51], scr[:, 56:57], ALU.subtract, ALU.mult)


def rope_apply(P, dst, src, cos_i, sin_i, nheads, scrA, scrB):
    hw = nheads * 32
    r4 = lambda a: a.rearrange("p (h d two) -> p h d two", two=2, d=32)
    se = r4(src)[:, :, :, 0:1]
    so = r4(src)[:, :, :, 1:2]
    de = r4(dst)[:, :, :, 0:1]
    do = r4(dst)[:, :, :, 1:2]
    cosb = cos_i.unsqueeze(1).broadcast_to([128, nheads, 32]).unsqueeze(3)
    sinb = sin_i.unsqueeze(1).broadcast_to([128, nheads, 32]).unsqueeze(3)
    A = scrA[:, 0:hw].rearrange("p (h d) -> p h d", d=32).unsqueeze(3)
    B = scrB[:, 0:hw].rearrange("p (h d) -> p h d", d=32).unsqueeze(3)
    e_tt(P, A, se, cosb, ALU.mult)
    e_tt(P, B, so, sinb, ALU.mult)
    e_tt(P, de, A, B, ALU.subtract)
    e_tt(P, A, se, sinb, ALU.mult)
    e_tt(P, B, so, cosb, ALU.mult)
    e_tt(P, do, A, B, ALU.add)


def qk_unitnorm(P, x_ap, nheads, scr, sq_scr):
    w = nheads * 64
    nh = nheads
    e_tt(P, sq_scr[:, 0:w], x_ap, x_ap, ALU.mult)
    u = scr[:, 0:nh]
    e_red(P, u, sq_scr[:, 0:w].rearrange("p (h d) -> p h d", d=64), ALU.add)
    s0 = scr[:, 8:8 + nh]
    r0 = scr[:, 16:16 + nh]
    t1 = scr[:, 24:24 + nh]
    e_sqrt(P, s0, u)
    e_ts(P, s0, s0, 1e-30, None, ALU.max)
    e_recip(P, r0, s0)
    e_tt(P, t1, u, r0, ALU.mult)
    e_tt(P, t1, t1, s0, ALU.add)
    e_ts(P, t1, t1, 0.5, None, ALU.mult)
    e_ts(P, t1, t1, 1e-12, None, ALU.max)
    e_recip(P, r0, t1)
    rcb = r0.unsqueeze(2).broadcast_to([128, nh, 64])
    x3 = x_ap.rearrange("p (h d) -> p h d", d=64)
    e_tt(P, x3, x3, rcb, ALU.mult)


def round_ladder(P, dst, src, scale_ap, f32_scr, m_scr, i32_scr, hr=6):
    e_ts(P, f32_scr, src, scale_ap, 200.5, ALU.mult, ALU.add)
    e_copy(P, "vector", i32_scr, f32_scr)
    e_tt(P, dst, f32_scr, i32_scr, ALU.subtract)
    e_ts(P, m_scr, dst, float(-hr), None, ALU.is_ge)
    for k in range(-hr + 1, hr + 1):
        e_stt(P, m_scr, dst, float(k), m_scr, ALU.is_ge, ALU.add)
    e_stt(P, dst, i32_scr, 200.0 + hr + 1.0, m_scr, ALU.subtract, ALU.add)


def round_halfup(P, dst, src, scale_ap, f32_scr, m_scr, i32_scr):
    e_ts(P, f32_scr, src, scale_ap, 200.5, ALU.mult, ALU.add)
    e_copy(P, "vector", i32_scr, f32_scr)
    e_tt(P, m_scr, i32_scr, f32_scr, ALU.is_gt)
    e_stt(P, dst, i32_scr, 200.0, m_scr, ALU.subtract, ALU.subtract)
    e_stt(P, m_scr, i32_scr, 1.0, f32_scr, ALU.add, ALU.is_le)
    e_tt(P, dst, dst, m_scr, ALU.add)
    e_stt(P, m_scr, dst, 200.0, f32_scr, ALU.add, ALU.is_gt)
    e_tt(P, dst, dst, m_scr, ALU.subtract)
    e_stt(P, m_scr, dst, 201.0, f32_scr, ALU.add, ALU.is_le)
    e_tt(P, dst, dst, m_scr, ALU.add)


def e_rsqrt_newton(P, dst, u, scr, c0):
    s0 = scr[:, c0:c0 + 1]
    r0 = scr[:, c0 + 1:c0 + 2]
    t1 = scr[:, c0 + 2:c0 + 3]
    e_sqrt(P, s0, u)
    e_recip(P, r0, s0)
    e_tt(P, t1, r0, r0, ALU.mult)
    e_tt(P, t1, t1, u, ALU.mult)
    e_ts(P, t1, t1, -0.5, 1.5, ALU.mult, ALU.add)
    e_tt(P, dst, r0, t1, ALU.mult)


def e_sqrt_newton(P, dst, u, scr, c0):
    s0 = scr[:, c0:c0 + 1]
    r0 = scr[:, c0 + 1:c0 + 2]
    t1 = scr[:, c0 + 2:c0 + 3]
    e_sqrt(P, s0, u)
    e_ts(P, s0, s0, 1e-30, None, ALU.max)
    e_recip(P, r0, s0)
    e_tt(P, t1, u, r0, ALU.mult)
    e_tt(P, t1, t1, s0, ALU.add)
    e_ts(P, dst, t1, 0.5, None, ALU.mult)



def build_program(n_stages=None):
    if n_stages is None:
        n_stages = N_STAGES
    nc = bass.Bass("TRN2", target_bir_lowering=False)

    def din(name, shape, dtype=F32):
        return nc.dram_tensor(name, list(shape), dtype, kind="ExternalInput").ap()

    x_in = din("x", (S, D))
    out_t = nc.dram_tensor("out", [S, D], F32, kind="ExternalOutput").ap()

    wqkvT0 = din("wqkvT0", (D, 3 * D))
    wprojT0 = din("wprojT0", (D, D))
    w1T = din("w1T", (D, DFF))
    w2T = din("w2T", (DFF, D))
    b1bc = din("b1bc", (128, DFF))
    b2bc = din("b2bc", (128, D))
    wqkvT1 = din("wqkvT1", (D, 3 * D))
    bqkv1bc = din("bqkv1bc", (128, 3 * D))
    woT1 = din("woT1", (D, D))
    bo1bc = din("bo1bc", (128, D))
    wT_l1f = din("wT_l1f", (D, D))
    bl1f_bc = din("bl1f_bc", (128, D))
    lora_downT = din("lora_downT", (100, 4))
    lora_upT = din("lora_upT", (4, 50))
    down_aux = din("down_aux", (100, D))
    up_auxT = din("up_auxT", (50, D))
    identf = din("identf", (128, 128))
    identb = din("identb", (128, 128), BF16)
    cmask = din("cmask", (128, 128))
    ropeC = din("ropeC", (S, 32))
    ropeS = din("ropeS", (S, 32))
    tgtc = din("tgtc", (128, NT))
    iota8 = din("iota8", (128, 8))
    w1q_d = nc.dram_tensor("w1q_d", [D, DFF], BF16, kind="Internal").ap()
    w2q_d = nc.dram_tensor("w2q_d", [DFF, D], BF16, kind="Internal").ap()

    def tap(name, ap):
        if name not in TAPS:
            return
        t = nc.dram_tensor("tap_" + name, list(ap.shape), ap.dtype,
                           kind="ExternalOutput").ap()
        P.dma("dse", t, ap)

    es = ExitStack()
    with es:
        P = Prog(nc, es)
        sbg = lambda nm, shp, dt=F32: es.enter_context(nc.sbuf_tensor(nm, shp, dt))[:]

        X = []
        IDF = sbg("IDF", [128, 128])
        IDB = sbg("IDB", [128, 128], BF16)
        CMs = sbg("CMs", [128, 128])
        TCs = sbg("TCs", [128, NT])
        IO8 = sbg("IO8", [128, 8])
        GSM = sbg("GSM", [128, 64])

        if n_stages == 0:
            X.extend(sbg(f"X{i}", [128, D]) for i in range(NT))
            for i in range(NT):
                P.dma("dge", X[i], x_in[i * 128:(i + 1) * 128, :])
        P.dma("dsy", IDF, identf)
        P.dma("dsy", IDB, identb)
        P.dma("dsy", CMs, cmask)
        P.dma("dsy", TCs, tgtc)
        P.dma("dsy", IO8, iota8)
        P.flush_block()

        def attn_layer(layer):
            wqkvT = wqkvT0 if layer == 0 else wqkvT1
            wpT = wprojT0 if layer == 0 else woT1
            with ExitStack() as ls:
                sl_ = lambda nm, shp, dt=F32: ls.enter_context(
                    nc.sbuf_tensor(uname(nm), shp, dt, side="right"))[:]
                AVT = [sl_(f"AVT{c}", [128, S]) for c in range(8)]
                for hg in range(NHG):
                    with ExitStack() as hs:
                        s2 = lambda nm, shp, dt=F32: hs.enter_context(
                            nc.sbuf_tensor(uname(nm), shp, dt))[:]
                        p2 = lambda nm, shp: hs.enter_context(
                            nc.psum_tensor(uname(nm), shp, F32))[:]
                        WQ = [s2(f"WQ{p}", [128, 512]) for p in range(2)]
                        KT = [s2(f"KT{c}", [128, S]) for c in range(2)]
                        VS = [s2(f"VS{i}", [128, 256]) for i in range(NT)]
                        BQ = [s2(f"BQ{p}", [128, 256]) for p in range(2)] \
                            if layer == 1 else None
                        XTt = [[s2(f"XTt{t}_{f}", [128, 128]) for f in range(8)]
                               for t in range(2)]
                        QTt = [[s2(f"QTt{t}_{c}", [128, 128]) for c in range(2)]
                               for t in range(2)]
                        RW1 = s2("RW1", [128, 512])
                        QRr = s2("QRr", [128, 256])
                        RA = s2("RA", [128, 128])
                        RB = s2("RB", [128, 128])
                        RCB = [s2(f"RCB{t}", [128, 32]) for t in range(2)]
                        RSB = [s2(f"RSB{t}", [128, 32]) for t in range(2)]
                        if layer == 0:
                            SQs = s2("SQs", [128, 256])
                            NSM = s2("NSM", [128, 64])
                        nS = 4 if layer == 0 else 2
                        nP = 4 if layer == 0 else 1
                        Ssb = [s2(f"Ssb{p}", [128, S]) for p in range(nS)]
                        Psb = [s2(f"Psb{p}", [128, S]) for p in range(nP)]
                        PTT = [s2(f"PTT{p}", [128, 256 if layer == 0 else 128])
                               for p in range(2)]
                        STT = s2("STT", [128, 64])
                        if layer == 0:
                            J16 = s2("J16", [128, S], BF16)
                            JS16 = s2("JS16", [128, S], BF16)
                            M8 = s2("M8", [128, 64])
                            OH = s2("OH", [128, 64])
                        PPART = p2("PPART", [128, 1024])
                        PSs = [p2(f"PSs{p}", [128, 512]) for p in range(2)]
                        PTp = [p2(f"PTp{p}", [128, 128]) for p in range(2)]
                        PAV = [p2(f"PAV{p}", [64, 256]) for p in range(2)]

                        XIN = [s2(f"XIN{t}", [128, D]) for t in range(2)] \
                            if layer == 0 else None

                        def sbuf_idx(t, hl):
                            if layer == 0:
                                return t * 2 + (hl % 2)
                            return t

                        PENDAV = [None]
                        for ib in range(NT // 2):
                            tt2 = [ib * 2, ib * 2 + 1]
                            for t, i in enumerate(tt2):
                                P.dma("dge", RCB[t], ropeC[i * 128:(i + 1) * 128, :])
                                P.dma("dge", RSB[t], ropeS[i * 128:(i + 1) * 128, :])
                                if layer == 0:
                                    P.dma("dge", XIN[t],
                                          x_in[i * 128:(i + 1) * 128, :])
                                xi = XIN[t] if layer == 0 else X[i]
                                for f in range(8):
                                    e_trans(P, PTp[f % 2],
                                            xi[:, f * 128:(f + 1) * 128], IDF)
                                    e_copy(P, "scalar", XTt[t][f], PTp[f % 2])
                            c0q = hg * 256
                            c0k = D + hg * 256
                            for f in range(8):
                                wb = WQ[f % 2]
                                P.dma("dsy", wb[:, 0:256],
                                      wqkvT[f * 128:(f + 1) * 128, c0q:c0q + 256])
                                P.dma("dsy", wb[:, 256:512],
                                      wqkvT[f * 128:(f + 1) * 128, c0k:c0k + 256])
                                for t in range(2):
                                    e_mm(P, PPART[:, t * 512:(t + 1) * 512],
                                         XTt[t][f], wb, start=(f == 0), stop=(f == 7))
                            for t, i in enumerate(tt2):
                                for part in range(2):
                                    sl = PPART[:, t * 512 + part * 256:
                                               t * 512 + part * 256 + 256]
                                    e_copy(P, "scalar", RW1[:, part * 256:
                                                            part * 256 + 256], sl)
                                if layer == 1:
                                    P.dma("dsy", BQ[0], bqkv1bc[:, c0q:c0q + 256])
                                    P.dma("dsy", BQ[1], bqkv1bc[:, c0k:c0k + 256])
                                    e_tt(P, RW1[:, 0:256], RW1[:, 0:256],
                                         BQ[0], ALU.add)
                                    e_tt(P, RW1[:, 256:512], RW1[:, 256:512],
                                         BQ[1], ALU.add)
                                for part in range(2):
                                    rw = RW1[:, part * 256:part * 256 + 256]
                                    rope_apply(P, QRr, rw, RCB[t], RSB[t],
                                               HPG, RA, RB)
                                    if layer == 0:
                                        qk_unitnorm(P, QRr, HPG, NSM, SQs)
                                    for c in range(2):
                                        e_trans(P, PTp[c],
                                                QRr[:, c * 128:(c + 1) * 128], IDF)
                                        if part == 0:
                                            e_copy(P, "scalar", QTt[t][c], PTp[c])
                                        else:
                                            e_copy(P, "scalar",
                                                   KT[c][:, i * 128:(i + 1) * 128],
                                                   PTp[c])
                            c0v = 2 * D + hg * 256
                            for f in range(8):
                                wb = WQ[f % 2]
                                P.dma("dsy", wb[:, 0:256],
                                      wqkvT[f * 128:(f + 1) * 128, c0v:c0v + 256])
                                for t in range(2):
                                    e_mm(P, PPART[:, t * 512:t * 512 + 256],
                                         XTt[t][f], wb[:, 0:256],
                                         start=(f == 0), stop=(f == 7))
                            for t, i in enumerate(tt2):
                                e_copy(P, "scalar", VS[i],
                                       PPART[:, t * 512:t * 512 + 256])
                                if layer == 1:
                                    P.dma("dsy", BQ[t], bqkv1bc[:, c0v:c0v + 256])
                                    e_tt(P, VS[i], VS[i], BQ[t], ALU.add)

                            def mk_scores(t, i, hl):
                                si = sbuf_idx(t, hl)
                                K_i = (i + 1) * 128
                                ch, ro = hl // 2, 64 * (hl % 2)
                                qT = QTt[t][ch][ro:ro + 64, :]
                                nfull = (i * 128) // 512
                                blocks = [(b * 512, 512) for b in range(nfull)]
                                rem = (i * 128) % 512
                                if rem:
                                    blocks.append((nfull * 512, rem))
                                blocks.append((i * 128, 128))
                                for bi_, (bc0, bw) in enumerate(blocks):
                                    pb = PSs[bi_ % 2]
                                    e_mm(P, pb[:, 0:bw], qT,
                                         KT[ch][ro:ro + 64, bc0:bc0 + bw],
                                         start=True, stop=True)
                                    e_mul(P, Ssb[si][:, bc0:bc0 + bw],
                                          pb[:, 0:bw], 0.125)
                                e_tt(P, Ssb[si][:, i * 128:K_i],
                                     Ssb[si][:, i * 128:K_i], CMs, ALU.add)

                            def mk_softmax_av(t, i, hl, pslot):
                                si = sbuf_idx(t, hl)
                                K_i = (i + 1) * 128
                                col = t + 2 * (hl % 2)
                                z = STT[:, 24 + col:25 + col]
                                rz = STT[:, 28 + col:29 + col]
                                sA = Ssb[si][:, 0:K_i]
                                pb = Psb[pslot][:, 0:K_i]
                                if layer == 0:
                                    hi = STT[:, 16 + col:17 + col]
                                    chi = STT[:, 20 + col:21 + col]
                                    e_ts(P, J16[:, 0:K_i], sA, hi, None,
                                         ALU.is_gt, ALU.add, accum=chi)
                                    e_ts(P, J16[:, 0:K_i], sA, hi, -1000.0,
                                         ALU.is_gt, ALU.mult)
                                    e_tt(P, pb, J16[:, 0:K_i], sA, ALU.add)
                                    e_max8(P, M8[:, col * 8:(col + 1) * 8], pb)
                                    idx = STT[:, 32 + col:33 + col]
                                    e_tt(P, idx, STT[:, 12 + t:13 + t], chi,
                                         ALU.subtract)
                                    e_ts(P, idx, idx, 1.0, 0.0,
                                         ALU.subtract, ALU.max)
                                    e_ts(P, idx, idx, 7.0, None, ALU.min)
                                    e_tt(P, OH[:, col * 8:(col + 1) * 8], IO8,
                                         idx.broadcast_to([128, 8]), ALU.is_equal)
                                    e_tt(P, OH[:, col * 8:(col + 1) * 8],
                                         OH[:, col * 8:(col + 1) * 8],
                                         M8[:, col * 8:(col + 1) * 8], ALU.mult)
                                    tau = STT[:, 36 + col:37 + col]
                                    e_red(P, tau, OH[:, col * 8:(col + 1) * 8],
                                          ALU.add)
                                    e_act(P, pb, sA, AFT.Exp)
                                    e_stt(P, pb, sA, tau, pb,
                                          ALU.is_ge, ALU.mult, accum=z)
                                    e_recip(P, rz, z)
                                    e_act(P, pb, pb, AFT.Copy, scale=rz)
                                    if hg == 0 and i == 1:
                                        tap(f"selstt{hl}", STT[:, 0:20])
                                        tap(f"selchi{hl}", STT[:, 20 + col:21 + col])
                                        tap(f"seltau{hl}", STT[:, 36 + col:37 + col])
                                        tap(f"selidx{hl}", STT[:, 32 + col:33 + col])
                                        tap(f"selm8{hl}", M8[:, col * 8:(col + 1) * 8])
                                        tap(f"selp{hl}", Psb[pslot][:, 0:256])
                                        tap(f"selsA{hl}", Ssb[si][:, 0:256])
                                else:
                                    e_act(P, pb, sA, AFT.Exp, accum=z)
                                    e_recip(P, rz, z)
                                    e_act(P, pb, pb, AFT.Copy, scale=rz)

                            def mk_av_g(hl, i0, i1):
                                pav = PAV[hl % 2]
                                pb0 = Psb[hl % 2]
                                pb1 = Psb[2 + (hl % 2)] if len(Psb) > 2 else Psb[1]
                                for c in range(i0 + 1):
                                    par = c % 2
                                    e_trans(P, PTp[par],
                                            pb0[:, c * 128:(c + 1) * 128], IDF)
                                    e_copy(P, "scalar", PTT[par][:, 0:128], PTp[par])
                                    e_trans(P, PTp[par],
                                            pb1[:, c * 128:(c + 1) * 128], IDF)
                                    e_copy(P, "scalar", PTT[par][:, 128:256], PTp[par])
                                    e_mm(P, pav, VS[c][:, hl * 64:(hl + 1) * 64],
                                         PTT[par], start=(c == 0), stop=False)
                                    yield
                                par = (i0 + 1) % 2
                                e_trans(P, PTp[par],
                                        pb1[:, i1 * 128:(i1 + 1) * 128], IDF)
                                e_copy(P, "scalar", PTT[par][:, 128:256], PTp[par])
                                e_mm(P, pav[:, 128:256],
                                     VS[i1][:, hl * 64:(hl + 1) * 64],
                                     PTT[par][:, 128:256], start=False, stop=True)
                                e_copy(P, "scalar",
                                       AVT[hg * 2 + hl // 2]
                                       [(hl % 2) * 64:(hl % 2) * 64 + 64,
                                        i0 * 128:i0 * 128 + 256], pav)

                            def mk_av_pair(hl, i0, i1):
                                for _ in mk_av_g(hl, i0, i1):
                                    pass

                            def mk_select_g(pr):
                                he, ho = 2 * pr, 2 * pr + 1
                                tv = STT[:, 0:4]
                                cn = STT[:, 4:8]
                                ge = STT[:, 8:12]
                                tg = STT[:, 12:16]
                                hi4 = STT[:, 16:20]
                                for t, i in enumerate(tt2):
                                    e_copy(P, "vector", tg[:, t:t + 1],
                                           TCs[:, i:i + 1])
                                    e_ts(P, tg[:, 2 + t:3 + t], TCs[:, i:i + 1],
                                         2.0, -float((i + 1) * 128),
                                         ALU.mult, ALU.add)
                                e_memset(P, tv, 0.0)
                                ntv = STT[:, 40:42]
                                delta = SEL_DELTA0
                                for r in range(R_SEL):
                                    delta *= 0.5
                                    e_ts(P, ntv, tv[:, 2:4], -1.0, None, ALU.mult)
                                    for t, i in enumerate(tt2):
                                        K_i = (i + 1) * 128
                                        sie = sbuf_idx(t, he)
                                        sio = sbuf_idx(t, ho)
                                        e_ts(P, J16[:, 0:K_i],
                                             Ssb[sie][:, 0:K_i],
                                             tv[:, t:t + 1], None,
                                             ALU.is_gt, ALU.add,
                                             accum=cn[:, t:t + 1])
                                        def em_sign(E, d=JS16[:, 0:K_i],
                                                    s=Ssb[sio][:, 0:K_i],
                                                    b=ntv[:, t:t + 1],
                                                    ac=cn[:, 2 + t:3 + t]):
                                            return E.activation(
                                                out=d, in_=s, func=AFT.Sign,
                                                bias=b, scale=1.0,
                                                accum_out=ac)
                                        P.op("scalar", em_sign,
                                             reads=[Ssb[sio], STT],
                                             writes=[JS16, STT])
                                    e_tt(P, ge, cn, tg, ALU.is_ge)
                                    e_stt(P, tv, ge, 2.0 * delta, tv,
                                          ALU.mult, ALU.add)
                                    e_ts(P, tv, tv, delta, None, ALU.subtract)
                                    yield
                                e_ts(P, hi4, tv, delta, None, ALU.add)

                            _SENT = object()

                            def drive(selg, fillers):
                                fl = list(fillers)
                                for _ in selg:
                                    for g in list(fl):
                                        if next(g, _SENT) is _SENT:
                                            fl.remove(g)
                                for g in fl:
                                    for _ in g:
                                        pass

                            i0, i1 = tt2[0], tt2[1]
                            if layer == 0:
                                for t, i in enumerate(tt2):
                                    mk_scores(t, i, 0)
                                    mk_scores(t, i, 1)
                                fillers0 = []
                                if PENDAV[0] is not None:
                                    pi0, pi1 = PENDAV[0]
                                    fillers0 = [mk_av_g(2, pi0, pi1),
                                                mk_av_g(3, pi0, pi1)]
                                    PENDAV[0] = None
                                drive(mk_select_g(0), fillers0)
                                mk_softmax_av(0, i0, 0, 0)
                                mk_softmax_av(1, i1, 0, 2)
                                mk_softmax_av(0, i0, 1, 1)
                                mk_softmax_av(1, i1, 1, 3)
                                mk_scores(0, i0, 2)
                                mk_scores(1, i1, 2)
                                mk_scores(0, i0, 3)
                                mk_scores(1, i1, 3)
                                drive(mk_select_g(1),
                                      [mk_av_g(0, i0, i1), mk_av_g(1, i0, i1)])
                                mk_softmax_av(0, i0, 2, 0)
                                mk_softmax_av(1, i1, 2, 2)
                                mk_softmax_av(0, i0, 3, 1)
                                mk_softmax_av(1, i1, 3, 3)
                                PENDAV[0] = (i0, i1)
                            else:
                                def mk_av1(t, i, hl):
                                    pb = Psb[0]
                                    for c in range(i + 1):
                                        par = c % 2
                                        e_trans(P, PTp[par],
                                                pb[:, c * 128:(c + 1) * 128], IDF)
                                        e_copy(P, "scalar", PTT[par], PTp[par])
                                        e_mm(P, PAV[hl % 2][:, 0:128],
                                             VS[c][:, hl * 64:(hl + 1) * 64],
                                             PTT[par], start=(c == 0), stop=(c == i))
                                    e_copy(P, "scalar",
                                           AVT[hg * 2 + hl // 2]
                                           [(hl % 2) * 64:(hl % 2) * 64 + 64,
                                            i * 128:(i + 1) * 128],
                                           PAV[hl % 2][:, 0:128])
                                for hl in range(HPG):
                                    for t, i in enumerate(tt2):
                                        mk_scores(t, i, hl)
                                    mk_softmax_av(0, i0, hl, 0)
                                    mk_av1(0, i0, hl)
                                    mk_softmax_av(1, i1, hl, 0)
                                    mk_av1(1, i1, hl)
                        if layer == 0 and PENDAV[0] is not None:
                            pi0, pi1 = PENDAV[0]
                            mk_av_pair(2, pi0, pi1)
                            mk_av_pair(3, pi0, pi1)
                            PENDAV[0] = None
                        P.flush_block()
                for c_ in range(8):
                    tap(f"avt{layer}_{c_}", AVT[c_][:, 128:256])
                if layer == 0 and not X:
                    X.extend(sbg(f"X{i}", [128, D]) for i in range(NT))
                with ExitStack() as pj:
                    s3 = lambda nm, shp, dt=F32: pj.enter_context(
                        nc.sbuf_tensor(uname(nm), shp, dt))[:]
                    WPJ = [s3(f"WPJ{f}", [128, D]) for f in range(8)]
                    BO = s3("BO", [128, D]) if layer == 1 else None
                    PPJ = pj.enter_context(
                        nc.psum_tensor(uname("PPJ"), [128, D], F32))[:]
                    for f in range(8):
                        P.dma("dsy", WPJ[f], wpT[f * 128:(f + 1) * 128, :])
                    if layer == 1:
                        P.dma("dsy", BO, bo1bc)
                    for i in range(NT):
                        if layer == 0:
                            P.dma("dge", X[i], x_in[i * 128:(i + 1) * 128, :])
                        for f in range(8):
                            for o2 in range(2):
                                e_mm(P, PPJ[:, o2 * 512:(o2 + 1) * 512],
                                     AVT[f][:, i * 128:(i + 1) * 128],
                                     WPJ[f][:, o2 * 512:(o2 + 1) * 512],
                                     start=(f == 0), stop=(f == 7))
                        e_tt(P, X[i], X[i], PPJ, ALU.add)
                        if layer == 1:
                            e_tt(P, X[i], X[i], BO, ALU.add)
                        layernorm_inplace(P, X[i], GSM)
                    P.flush_block()

        def l0_ffn():
            with ExitStack() as ph0:
                RSW = ph0.enter_context(nc.sbuf_tensor(uname("RSW"), [128, 2], F32))[:]
                l0_ffn_quant(RSW)
                l0_ffn_main(RSW)

        def l0_ffn_quant(RSW):
            with ExitStack() as qs:
                s2 = lambda nm, shp, dt=F32: qs.enter_context(nc.sbuf_tensor(uname(nm), shp, dt))[:]
                WS = [s2(f"WS{p}", [128, DFF]) for p in range(2)]
                WQB = [s2(f"WQB{p}", [128, DFF], BF16) for p in range(2)]
                WI = s2("WI", [128, 512], I32)
                RSC = s2("RSC", [128, 512])
                RSM = s2("RSM", [128, 512])
                CS = s2("CS", [128, 64])
                ONE = s2("ONE", [128, 1])
                ONER = s2("ONER", [1, 128])
                PS1 = qs.enter_context(nc.psum_tensor(uname("PS1"), [128, 8], F32))[:]
                e_memset(P, ONE, 1.0)
                e_memset(P, ONER, 1.0)
                for (wsrc, wdst, nchunk, wid, csl) in (
                        (w1T, w1q_d, 8, DFF, 0), (w2T, w2q_d, 32, D, 1)):
                    acc = CS[:, 4 + csl:5 + csl]
                    e_memset(P, acc, 0.0)
                    for f in range(nchunk):
                        p = f % 2
                        P.dma("dsy", WS[p][:, 0:wid], wsrc[f * 128:(f + 1) * 128, :])
                        nseg = wid // 128
                        e_red(P, CS[:, 20:20 + nseg],
                              WS[p][:, 0:wid].rearrange("p (g k) -> p g k", k=128),
                              ALU.add, absval=True)
                        e_red(P, CS[:, p:p + 1], CS[:, 20:20 + nseg], ALU.add)
                        e_tt(P, acc, acc, CS[:, p:p + 1], ALU.add)
                    e_mm(P, PS1[0:1, 0:1], ONE, acc, start=True, stop=True)
                    e_copy(P, "vector", CS[0:1, 8 + csl:9 + csl], PS1[0:1, 0:1])
                    e_ts(P, CS[0:1, 10 + csl:11 + csl], CS[0:1, 8 + csl:9 + csl],
                         1.0 / (D * DFF), 1e-5, ALU.mult, ALU.max)
                    e_recip(P, CS[0:1, 12 + csl:13 + csl], CS[0:1, 10 + csl:11 + csl])
                    e_mm(P, PS1[:, 1:2], ONER,
                         CS[0:1, 12 + csl:13 + csl], start=True, stop=True)
                    e_copy(P, "vector", CS[:, 14 + csl:15 + csl], PS1[:, 1:2])
                    e_mm(P, PS1[:, 2:3], ONER,
                         CS[0:1, 10 + csl:11 + csl], start=True, stop=True)
                    e_copy(P, "vector", CS[:, 16 + csl:17 + csl], PS1[:, 2:3])
                    sw = CS[:, 14 + csl:15 + csl]
                    for f in range(nchunk):
                        p = f % 2
                        P.dma("dsy", WS[p][:, 0:wid], wsrc[f * 128:(f + 1) * 128, :])
                        for sl in range(0, wid, 512):
                            round_halfup(P, WQB[p][:, sl:sl + 512],
                                         WS[p][:, sl:sl + 512], sw, RSC, RSM, WI)
                        e_ts(P, WQB[p][:, 0:wid], WQB[p][:, 0:wid], -1.0, 1.0,
                             ALU.max, ALU.min)
                        P.dma("dge", wdst[f * 128:(f + 1) * 128, :], WQB[p][:, 0:wid])
                e_copy(P, "vector", RSW[:, 0:1], CS[:, 16:17])
                e_copy(P, "vector", RSW[:, 1:2], CS[:, 17:18])
                tap("ffn_cs", CS)
                P.flush_block()
                tap("ffn_w1q", w1q_d[0:128, :])

        def l0_ffn_main(RSW):
                rsw1g = RSW[:, 0:1]
                rsw2g = RSW[:, 1:2]
                with ExitStack() as ms:
                    s3 = lambda nm, shp, dt=F32: ms.enter_context(
                        nc.sbuf_tensor(uname(nm), shp, dt))[:]
                    B1W = [s3(f"B1W{p}", [128, 512]) for p in range(2)]
                    B2s = s3("B2s", [128, D])
                    P.dma("dsy", B2s, b2bc)
                    W1B = [s3(f"W1B{p}", [128, 512], BF16) for p in range(2)]
                    W2B = [s3(f"W2B{p}", [128, 512], BF16) for p in range(2)]
                    XQT = [[s3(f"XQT{t}_{f}", [128, 128], BF16) for f in range(8)]
                           for t in range(4)]
                    HQT = [[s3(f"HQT{t}_{f}", [128, 128], BF16) for f in range(32)]
                           for t in range(4)]
                    HSB = [s3(f"HSB{t}", [128, DFF]) for t in range(4)]
                    XN = s3("XN", [128, D])
                    XQ = s3("XQ", [128, D], BF16)
                    HNS = s3("HNS", [128, 512])
                    GSC = s3("GSC", [128, 512])
                    RSC2 = s3("RSC2", [128, 512])
                    RSM2 = s3("RSM2", [128, 512])
                    HQs = s3("HQs", [128, 512], BF16)
                    WI2 = s3("WI2", [128, 512], I32)
                    QS1 = s3("QS1", [128, 8])
                    QS2 = s3("QS2", [128, 8])
                    SM = s3("SM", [128, 32])
                    SMB = s3("SMB", [128, 64])
                    PH = [ms.enter_context(nc.psum_tensor(uname(f"PH{t}"), [128, 512], F32))[:]
                          for t in range(4)]
                    PTq = [ms.enter_context(nc.psum_tensor(uname(f"PTq{p}"), [128, 128], BF16))[:]
                           for p in range(2)]
                    for tg in range(4):
                        tt4 = [tg * 4 + t for t in range(4)]
                        for t, i in enumerate(tt4):
                            e_tt(P, HSB[t][:, 0:D], X[i], X[i], ALU.mult)
                            e_red(P, SM[:, 0:1], HSB[t][:, 0:D], ALU.add)
                            e_ts(P, SM[:, 1:2], SM[:, 0:1], 1.0 / D, 1e-6,
                                 ALU.mult, ALU.add)
                            e_rsqrt_newton(P, SM[:, 3:4], SM[:, 1:2], SMB, 60)
                            e_ts(P, XN, X[i], SM[:, 3:4], None, ALU.mult)
                            e_red(P, SM[:, 4:5], XN, ALU.max, absval=True)
                            e_ts(P, SM[:, 5:6], SM[:, 4:5], 1e-5, None, ALU.max)
                            e_recip(P, SM[:, 6:7], SM[:, 5:6])
                            e_ts(P, SM[:, 7:8], SM[:, 6:7], 127.0, None, ALU.mult)
                            for sl in range(0, D, 512):
                                round_halfup(P, XQ[:, sl:sl + 512], XN[:, sl:sl + 512],
                                             SM[:, 7:8], RSC2, RSM2, WI2)
                            e_ts(P, SM[:, 8:9], SM[:, 5:6], 1.0 / 127.0, None, ALU.mult)
                            e_tt(P, QS1[:, t:t + 1], SM[:, 8:9], rsw1g, ALU.mult)
                            if i == 0:
                                tap("ffn_xq", XQ)
                                tap("ffn_xn", XN)
                                pass
                            for f in range(8):
                                e_trans(P, PTq[f % 2], XQ[:, f * 128:(f + 1) * 128], IDB)
                                e_copy(P, "scalar", XQT[t][f], PTq[f % 2])
                        for ow in range(8):
                            for f in range(8):
                                wb = W1B[f % 2]
                                P.dma("dsy", wb, w1q_d[f * 128:(f + 1) * 128,
                                                      ow * 512:(ow + 1) * 512])
                                for t in range(4):
                                    e_mm(P, PH[t], XQT[t][f], wb,
                                         start=(f == 0), stop=(f == 7))
                            P.dma("dse", B1W[ow % 2], b1bc[:, ow * 512:(ow + 1) * 512])
                            for t in range(4):
                                hsl = HSB[t][:, ow * 512:(ow + 1) * 512]
                                e_stt(P, hsl, PH[t],
                                      QS1[:, t:t + 1], B1W[ow % 2], ALU.mult, ALU.add)
                                e_act(P, hsl, hsl, AFT.Gelu)
                        if tg == 0:
                            tap("ffn_h0", HSB[0])
                            tap("ffn_qs1", QS1[:, 0:4])
                        for t, i in enumerate(tt4):
                            bn_meanvar(P, HSB[t], SMB, 50, 51)
                            e_tt(P, SM[:, 10:11], SMB[:, 50:51], SMB[:, 50:51], ALU.mult)
                            e_tt(P, SM[:, 11:12], SMB[:, 51:52], SM[:, 10:11], ALU.add)
                            e_ts(P, SM[:, 12:13], SM[:, 11:12], 1e-6, None, ALU.add)
                            e_rsqrt_newton(P, SM[:, 14:15], SM[:, 12:13], SMB, 60)
                            for sl in range(8):
                                e_ts(P, HNS, HSB[t][:, sl * 512:(sl + 1) * 512],
                                     SM[:, 14:15], None, ALU.mult)
                                e_red(P, SM[:, 16 + sl:17 + sl], HNS, ALU.max, absval=True)
                            e_red(P, SM[:, 24:25], SM[:, 16:24], ALU.max)
                            e_ts(P, SM[:, 25:26], SM[:, 24:25], 1e-5, None, ALU.max)
                            e_recip(P, SM[:, 26:27], SM[:, 25:26])
                            e_ts(P, SM[:, 27:28], SM[:, 26:27], 127.0, None, ALU.mult)
                            e_ts(P, SM[:, 28:29], SM[:, 25:26], 1.0 / 127.0, None, ALU.mult)
                            e_tt(P, QS2[:, t:t + 1], SM[:, 28:29], rsw2g, ALU.mult)
                            for sl in range(8):
                                e_ts(P, HNS, HSB[t][:, sl * 512:(sl + 1) * 512],
                                     SM[:, 14:15], None, ALU.mult)
                                round_ladder(P, HQs, HNS, SM[:, 27:28], RSC2, RSM2, WI2)
                                if tg == 0 and t == 0 and sl == 0:
                                    tap("ffn_hq0", HQs)
                                    tap("ffn_hn0", HNS)
                                    tap("ffn_s2", SM[:, 27:28])
                                for f2 in range(4):
                                    e_trans(P, PTq[f2 % 2], HQs[:, f2 * 128:(f2 + 1) * 128],
                                            IDB)
                                    e_copy(P, "scalar", HQT[t][sl * 4 + f2], PTq[f2 % 2])
                        for ow in range(2):
                            for f2 in range(32):
                                wb = W2B[f2 % 2]
                                P.dma("dsy", wb, w2q_d[f2 * 128:(f2 + 1) * 128,
                                                      ow * 512:(ow + 1) * 512])
                                for t in range(4):
                                    e_mm(P, PH[t], HQT[t][f2], wb,
                                         start=(f2 == 0), stop=(f2 == 31))
                            for t, i in enumerate(tt4):
                                e_stt(P, HSB[t][:, ow * 512:(ow + 1) * 512], PH[t],
                                      QS2[:, t:t + 1], B2s[:, ow * 512:(ow + 1) * 512],
                                      ALU.mult, ALU.add)
                        for t, i in enumerate(tt4):
                            e_tt(P, X[i], X[i], HSB[t][:, 0:D], ALU.add)
                            layernorm_inplace(P, X[i], GSM)
                    P.flush_block()

        def l1_ffn():
            with ExitStack() as ph:
                s2 = lambda nm, shp, dt=F32: ph.enter_context(nc.sbuf_tensor(uname(nm), shp, dt))[:]
                LDT = s2("LDT", [100, 4])
                LUT = s2("LUT", [4, 50])
                DAX = s2("DAX", [100, D])
                UAT = s2("UAT", [50, D])
                T1 = s2("T1", [4, D])
                T2 = s2("T2", [50, D])
                T3 = s2("T3", [128, D])
                WTOT = [s2(f"WTOT{f}", [128, D]) for f in range(8)]
                BL = s2("BL", [128, D])
                XTf = [s2(f"XTf{p}", [128, 128]) for p in range(2)]
                FO = s2("FO", [128, D])
                PL = ph.enter_context(nc.psum_tensor(uname("PL"), [128, D], F32))[:]
                PT2 = [ph.enter_context(nc.psum_tensor(uname(f"PT2{p}"), [128, 128], F32))[:]
                       for p in range(2)]
                P.dma("dsy", LDT, lora_downT)
                P.dma("dsy", LUT, lora_upT)
                P.dma("dsy", DAX, down_aux)
                P.dma("dsy", UAT, up_auxT)
                P.dma("dsy", BL, bl1f_bc)
                for f in range(8):
                    P.dma("dsy", WTOT[f], wT_l1f[f * 128:(f + 1) * 128, :])
                for o2 in range(2):
                    e_mm(P, PL[0:4, o2 * 512:(o2 + 1) * 512], LDT,
                         DAX[:, o2 * 512:(o2 + 1) * 512], start=True, stop=True)
                e_copy(P, "vector", T1, PL[0:4, :])
                for o2 in range(2):
                    e_mm(P, PL[0:50, o2 * 512:(o2 + 1) * 512], LUT,
                         T1[:, o2 * 512:(o2 + 1) * 512], start=True, stop=True)
                e_copy(P, "vector", T2, PL[0:50, :])
                for oc in range(8):
                    for o2 in range(2):
                        e_mm(P, PL[:, o2 * 512:(o2 + 1) * 512],
                             UAT[:, oc * 128:(oc + 1) * 128],
                             T2[:, o2 * 512:(o2 + 1) * 512], start=True, stop=True)
                    e_copy(P, "vector", T3, PL)
                    for f in range(8):
                        e_trans(P, PT2[f % 2], T3[:, f * 128:(f + 1) * 128], IDF)
                        e_tt(P, WTOT[f][:, oc * 128:(oc + 1) * 128],
                             WTOT[f][:, oc * 128:(oc + 1) * 128], PT2[f % 2], ALU.add)
                for i in range(NT):
                    for f in range(8):
                        e_trans(P, PT2[f % 2], X[i][:, f * 128:(f + 1) * 128], IDF)
                        e_copy(P, "scalar", XTf[f % 2], PT2[f % 2])
                        for o2 in range(2):
                            e_mm(P, PL[:, o2 * 512:(o2 + 1) * 512], XTf[f % 2],
                                 WTOT[f][:, o2 * 512:(o2 + 1) * 512],
                                 start=(f == 0), stop=(f == 7))
                    e_tt(P, FO, PL, BL, ALU.add)
                    e_tt(P, X[i], X[i], FO, ALU.add)
                    layernorm_inplace(P, X[i], GSM)
                P.flush_block()

        if n_stages >= 1:
            attn_layer(0)
        if n_stages >= 2:
            l0_ffn()
        if n_stages >= 3:
            attn_layer(1)
        if n_stages >= 4:
            l1_ffn()

        for i in range(NT):
            P.dma("dge", out_t[i * 128:(i + 1) * 128, :], X[i])
        P.wait_all_dma("gpsimd")
        P.flush_block()

    return nc



def _host_consts():
    c = {}
    c["identf"] = np.eye(128, dtype=np.float32)
    c["identb"] = np.eye(128, dtype=np.float32).astype(ml_dtypes.bfloat16)
    m = np.zeros((128, 128), np.float32)
    m[np.triu_indices(128, 1)] = NEG
    c["cmask"] = m
    import jax
    import jax.numpy as jnp
    with jax.default_device(jax.devices("cpu")[0]):
        inv = 1.0 / 10000.0 ** (jnp.arange(0, HD, 2, dtype=jnp.float32) / HD)
        tpos = jnp.arange(S, dtype=jnp.float32)
        fr = jnp.einsum("i,j->ij", tpos, inv)
        c["ropeC"] = np.asarray(jnp.cos(fr), dtype=np.float32)
        c["ropeS"] = np.asarray(jnp.sin(fr), dtype=np.float32)
    q = (np.arange(NT)[None, :] * 128 + np.arange(128)[:, None]).astype(np.float64)
    c["tgtc"] = np.minimum(q + 1.0, 128.0).astype(np.float32)
    c["iota8"] = np.broadcast_to(np.arange(8, dtype=np.float32), (128, 8)).copy()
    return c


def _host_tensors(inputs):
    T = {}
    f = np.float32
    cc = np.ascontiguousarray
    T["wqkvT0"] = cc(np.asarray(inputs["L0_attn_w_qkv"]).T.astype(f))
    T["wprojT0"] = cc(np.asarray(inputs["L0_attn_w_proj"]).T.astype(f))
    T["w1T"] = cc(np.asarray(inputs["L0_ffn_w1"]).T.astype(f))
    T["w2T"] = cc(np.asarray(inputs["L0_ffn_w2"]).T.astype(f))
    T["b1bc"] = cc(np.broadcast_to(np.asarray(inputs["L0_ffn_b1"]).astype(f), (128, DFF)))
    T["b2bc"] = cc(np.broadcast_to(np.asarray(inputs["L0_ffn_b2"]).astype(f), (128, D)))
    T["wqkvT1"] = cc(np.concatenate(
        [np.asarray(inputs["L1_attn_Wq"]).T, np.asarray(inputs["L1_attn_Wk"]).T,
         np.asarray(inputs["L1_attn_Wv"]).T], axis=1).astype(f))
    T["bqkv1bc"] = cc(np.broadcast_to(np.concatenate(
        [np.asarray(inputs["L1_attn_bq"]), np.asarray(inputs["L1_attn_bk"]),
         np.asarray(inputs["L1_attn_bv"])]).astype(f), (128, 3 * D)))
    T["woT1"] = cc(np.asarray(inputs["L1_attn_Wo"]).T.astype(f))
    T["bo1bc"] = cc(np.broadcast_to(np.asarray(inputs["L1_attn_bo"]).astype(f), (128, D)))
    T["wT_l1f"] = cc(np.asarray(inputs["L1_ffn_W"]).T.astype(f))
    T["bl1f_bc"] = cc(np.broadcast_to(np.asarray(inputs["L1_ffn_b"]).astype(f), (128, D)))
    T["lora_downT"] = cc(np.asarray(inputs["L1_ffn_lora_down"]).T.astype(f))
    T["lora_upT"] = cc(np.asarray(inputs["L1_ffn_lora_up"]).T.astype(f))
    T["down_aux"] = cc(np.asarray(inputs["L1_ffn_down_aux"]).astype(f))
    T["up_auxT"] = cc(np.asarray(inputs["L1_ffn_up_aux"]).T.astype(f))
    T.update(_host_consts())
    return T


def kernel(**inputs):
    x = np.asarray(inputs["L0_attn_x"], dtype=np.float32)
    T = _host_tensors(inputs)
    nc = build_program(N_STAGES)
    in_maps = []
    for c in range(N_CORES):
        m = dict(T)
        m["x"] = np.ascontiguousarray(x[c])
        in_maps.append(m)
    res = run_bass_kernel_spmd(nc, in_maps, core_ids=list(range(N_CORES)))
    return np.stack([r["out"] for r in res.results], axis=0)


# version 11 (final)
# speedup vs baseline: 1.0907x; 1.0147x over previous
import numpy as np
from contextlib import ExitStack
import concourse.bass as bass
import concourse.mybir as mybir
from concourse.bass_utils import run_bass_kernel_spmd
import ml_dtypes

F32 = mybir.dt.float32
BF16 = mybir.dt.bfloat16
I32 = mybir.dt.int32
ALU = mybir.AluOpType
AFT = mybir.ActivationFunctionType
AXX = mybir.AxisListType.X
AP = bass.AP

N_CORES = 8
S, D = 2048, 1024
NT = S // 128
NH, HD = 16, 64
HPG = 4
NHG = NH // HPG
DFF = 4096
NEG = -1.0e30
R_SEL = 13
SEL_DELTA0 = 0.13

N_STAGES = 4
TAPS = []

ENGS = ("tensor", "vector", "scalar", "gpsimd", "sync")
DMAQ = {"dsy": "sync", "dge": "gpsimd", "dse": "scalar"}


_UNIQ = [0]


def uname(nm):
    _UNIQ[0] += 1
    return f"{nm}_u{_UNIQ[0]}"


def bname(x):
    return x if isinstance(x, str) else x.tensor.name


def nc_sem(nc, name):
    return nc.semaphore(name)


class Prog:
    def __init__(self, nc, es):
        self.nc = nc
        self.es = es
        self.semh = {}
        for e in ENGS:
            self.semh[e] = es.enter_context(nc.semaphore(f"sp_{e}"))
        self.count = {e: 0 for e in ENGS}
        self.dcount = {}
        self.ops = {e: [] for e in ENGS}
        self.buf = {}
        self.watermark = {e: {} for e in ENGS}
        self.covered = {}

    def _dma_role(self, name):
        base = name.split("_u")[0]
        role = "dma_" + base
        if role not in self.semh:
            self.semh[role] = self.es.enter_context(nc_sem(self.nc, role))
            self.dcount[role] = 0
        return role

    def _b(self, name):
        return self.buf.setdefault(name, dict(w=None, r=[]))

    def _edges(self, reads, writes, waw):
        ed = []
        for nm in reads:
            b = self._b(nm)
            if b["w"] is not None:
                ed.append(b["w"])
        for nm in writes:
            b = self._b(nm)
            if waw and b["w"] is not None:
                ed.append(b["w"])
            ed.extend(b["r"])
        return ed

    def _filter(self, eng, ed):
        m = {}
        for s, v in ed:
            if m.get(s, -1) < v:
                m[s] = v
        w = self.watermark[eng]
        out = []
        for s, v in sorted(m.items()):
            if self.covered.get(s, 0) >= v or w.get(s, 0) >= v:
                continue
            out.append((s, v))
            w[s] = v
        return out

    def _update(self, reads, writes, ident):
        for nm in writes:
            b = self._b(nm)
            b["w"] = ident
            b["r"] = []
        for nm in reads:
            self._b(nm)["r"].append(ident)

    def op(self, eng, emit, reads=(), writes=(), waw=True):
        reads = [bname(a) for a in reads]
        writes = [bname(a) for a in writes]
        waits = self._filter(eng, self._edges(reads, writes, waw))
        self.count[eng] += 1
        ident = (eng, self.count[eng])
        self.ops[eng].append((waits, emit, (eng, 1)))
        self._update(reads, writes, ident)
        return ident

    def dma(self, q, out_ap, in_ap):
        eng = DMAQ[q]
        role = self._dma_role(bname(out_ap))
        reads = [bname(in_ap)]
        writes = [bname(out_ap)]
        waits = self._filter(eng, self._edges(reads, writes, True))
        self.dcount[role] += 1
        ident = (role, 16 * self.dcount[role])
        self.ops[eng].append(
            (waits, (lambda E, o=out_ap, i=in_ap: E.dma_start(out=o, in_=i)),
             (role, 16)))
        self._update(reads, writes, ident)
        return ident

    def wait_all_dma(self, eng):
        ed = [(r, 16 * n) for r, n in self.dcount.items() if n > 0]
        waits = self._filter(eng, ed)
        if waits:
            self.ops[eng].append((waits, None, None))

    def flush_block(self):
        nc = self.nc
        if not any(self.ops[e] for e in ENGS):
            return
        if not hasattr(self, "_bar_counts"):
            self._bar_counts = {}
        targets = []
        for e in ENGS:
            if self.count[e] > self._bar_counts.get(e, 0):
                targets.append((e, self.count[e]))
        for r, n in self.dcount.items():
            if 16 * n > self._bar_counts.get(r, 0):
                targets.append((r, 16 * n))
        for e in ENGS:
            waits = [(s, v) for (s, v) in targets]
            if waits:
                self.ops[e].append((waits, None, None))
        for s, v in targets:
            self._bar_counts[s] = v
        with nc.Block() as blk:
            for eng in ENGS:
                if not self.ops[eng]:
                    continue

                def mk(lst):
                    def emit_all(E):
                        for waits, emitf, inc in lst:
                            for (s, v) in waits:
                                E.wait_ge(self.semh[s], v)
                            if emitf is not None:
                                ins = emitf(E)
                                if inc is not None:
                                    ins.then_inc(self.semh[inc[0]], inc[1])
                    return emit_all
                getattr(blk, eng)(mk(self.ops[eng]))
        self.ops = {e: [] for e in ENGS}
        for e in ENGS:
            self.covered[e] = self.count[e]
        for r, n in self.dcount.items():
            self.covered[r] = 16 * n
        self.watermark = {e: {} for e in ENGS}



def e_copy(P, eng, dst, src):
    if eng == "scalar":
        return P.op(eng, lambda E, d=dst, s=src: E.copy(out=d, in_=s),
                    reads=[src], writes=[dst])
    return P.op(eng, lambda E, d=dst, s=src: E.tensor_copy(out=d, in_=s),
                reads=[src], writes=[dst])


def e_mul(P, dst, src, c):
    return P.op("scalar", lambda E, d=dst, s=src, m=c: E.mul(out=d, in_=s, mul=m),
                reads=[src], writes=[dst])


def e_mm(P, psum, lhsT, rhs, start, stop):
    return P.op("tensor",
                lambda E, o=psum, l=lhsT, r=rhs, st=start, sp=stop:
                E.matmul(o, l, r, start=st, stop=sp),
                reads=[lhsT, rhs], writes=[psum], waw=start)


def e_trans(P, psum, src, ident):
    return P.op("tensor", lambda E, o=psum, s=src, i=ident: E.transpose(o, s, i),
                reads=[src, ident], writes=[psum])


def e_ts(P, dst, src, s1, s2, op0, op1=None, accum=None, eng="vector"):
    def em(E, d=dst, s=src, a=s1, b=s2, o0=op0, o1=op1, ac=accum):
        kw = {}
        if o1 is not None:
            kw["op1"] = o1
        if ac is not None:
            kw["accum_out"] = ac
        return E.tensor_scalar(out=d, in0=s, scalar1=a, scalar2=b, op0=o0, **kw)
    reads = [src] + [x for x in (s1, s2) if isinstance(x, AP)]
    writes = [dst] + ([accum] if accum is not None else [])
    return P.op(eng, em, reads=reads, writes=writes)


def e_tt(P, dst, a, b, op, eng="vector"):
    return P.op(eng, lambda E, d=dst, x=a, y=b, o=op:
                E.tensor_tensor(out=d, in0=x, in1=y, op=o),
                reads=[a, b], writes=[dst])


def e_stt(P, dst, in0, scalar, in1, op0, op1, accum=None):
    def em(E, d=dst, i0=in0, sc=scalar, i1=in1, o0=op0, o1=op1, ac=accum):
        kw = {"accum_out": ac} if ac is not None else {}
        return E.scalar_tensor_tensor(out=d, in0=i0, scalar=sc, in1=i1,
                                      op0=o0, op1=o1, **kw)
    reads = [in0, in1] + ([scalar] if isinstance(scalar, AP) else [])
    writes = [dst] + ([accum] if accum is not None else [])
    return P.op("vector", em, reads=reads, writes=writes)


def e_red(P, dst, src, op, absval=False):
    def em(E, d=dst, s=src, o=op, av=absval):
        kw = {"apply_absolute_value": True} if av else {}
        return E.tensor_reduce(out=d, in_=s, axis=AXX, op=o, **kw)
    return P.op("vector", em, reads=[src], writes=[dst])


def e_act(P, dst, src, func, bias=None, scale=None, accum=None):
    def em(E, d=dst, s=src, f=func, b=bias, sc=scale, ac=accum):
        kw = {}
        if b is not None:
            kw["bias"] = b
        if sc is not None:
            kw["scale"] = sc
        if ac is not None:
            kw["accum_out"] = ac
        return E.activation(out=d, in_=s, func=f, **kw)
    reads = [src] + [x for x in (bias, scale) if isinstance(x, AP)]
    writes = [dst] + ([accum] if accum is not None else [])
    return P.op("scalar", em, reads=reads, writes=writes)


def e_max8(P, dst, src):
    return P.op("vector", lambda E, d=dst, s=src: E.max(out=d, in_=s),
                reads=[src], writes=[dst])


def e_recip(P, dst, src):
    return P.op("vector", lambda E, d=dst, s=src: E.reciprocal(out=d, in_=s),
                reads=[src], writes=[dst])


def e_sqrt(P, dst, src):
    return P.op("scalar", lambda E, d=dst, s=src: E.sqrt(out=d, in_=s),
                reads=[src], writes=[dst])


def e_memset(P, dst, val):
    return P.op("vector", lambda E, d=dst, v=val: E.memset(d, v),
                reads=[], writes=[dst])



def bn_meanvar(P, x_ap, scr, mcol, vcol):
    w = x_ap.shape[-1]
    k = (w + 511) // 512
    for j in range(k):
        P.op("vector", lambda E, d=scr[:, 6 * j:6 * j + 6],
             s=x_ap[:, j * 512:min((j + 1) * 512, w)]: E.bn_stats(out=d, in_=s),
             reads=[x_ap], writes=[scr])
    P.op("vector", lambda E, d=scr[:, mcol:vcol + 1], s=scr[:, 0:6 * k]:
         E.bn_aggr(out=d, in_=s), reads=[scr], writes=[scr])


def layernorm_inplace(P, x_ap, scr):
    bn_meanvar(P, x_ap, scr, 50, 51)
    e_ts(P, scr[:, 52:53], scr[:, 51:52], 1e-5, None, ALU.add)
    e_rsqrt_newton(P, scr[:, 56:57], scr[:, 52:53], scr, 57)
    e_ts(P, x_ap, x_ap, scr[:, 50:51], scr[:, 56:57], ALU.subtract, ALU.mult)


def rope_apply(P, dst, src, cos_i, sin_i, nheads, scrA, scrB):
    hw = nheads * 32
    r4 = lambda a: a.rearrange("p (h d two) -> p h d two", two=2, d=32)
    se = r4(src)[:, :, :, 0:1]
    so = r4(src)[:, :, :, 1:2]
    de = r4(dst)[:, :, :, 0:1]
    do = r4(dst)[:, :, :, 1:2]
    cosb = cos_i.unsqueeze(1).broadcast_to([128, nheads, 32]).unsqueeze(3)
    sinb = sin_i.unsqueeze(1).broadcast_to([128, nheads, 32]).unsqueeze(3)
    A = scrA[:, 0:hw].rearrange("p (h d) -> p h d", d=32).unsqueeze(3)
    B = scrB[:, 0:hw].rearrange("p (h d) -> p h d", d=32).unsqueeze(3)
    e_tt(P, A, se, cosb, ALU.mult)
    e_tt(P, B, so, sinb, ALU.mult)
    e_tt(P, de, A, B, ALU.subtract)
    e_tt(P, A, se, sinb, ALU.mult)
    e_tt(P, B, so, cosb, ALU.mult)
    e_tt(P, do, A, B, ALU.add)


def qk_unitnorm(P, x_ap, nheads, scr, sq_scr):
    w = nheads * 64
    nh = nheads
    e_tt(P, sq_scr[:, 0:w], x_ap, x_ap, ALU.mult)
    u = scr[:, 0:nh]
    e_red(P, u, sq_scr[:, 0:w].rearrange("p (h d) -> p h d", d=64), ALU.add)
    s0 = scr[:, 8:8 + nh]
    r0 = scr[:, 16:16 + nh]
    t1 = scr[:, 24:24 + nh]
    e_sqrt(P, s0, u)
    e_ts(P, s0, s0, 1e-30, None, ALU.max)
    e_recip(P, r0, s0)
    e_tt(P, t1, u, r0, ALU.mult)
    e_tt(P, t1, t1, s0, ALU.add)
    e_ts(P, t1, t1, 0.5, None, ALU.mult)
    e_ts(P, t1, t1, 1e-12, None, ALU.max)
    e_recip(P, r0, t1)
    rcb = r0.unsqueeze(2).broadcast_to([128, nh, 64])
    x3 = x_ap.rearrange("p (h d) -> p h d", d=64)
    e_tt(P, x3, x3, rcb, ALU.mult)


def round_ladder(P, dst, src, scale_ap, f32_scr, m_scr, i32_scr, hr=6):
    e_ts(P, f32_scr, src, scale_ap, 200.5, ALU.mult, ALU.add)
    e_copy(P, "vector", i32_scr, f32_scr)
    e_tt(P, dst, f32_scr, i32_scr, ALU.subtract)
    e_ts(P, m_scr, dst, float(-hr), None, ALU.is_ge)
    for k in range(-hr + 1, hr + 1):
        e_stt(P, m_scr, dst, float(k), m_scr, ALU.is_ge, ALU.add)
    e_stt(P, dst, i32_scr, 200.0 + hr + 1.0, m_scr, ALU.subtract, ALU.add)


def round_halfup(P, dst, src, scale_ap, f32_scr, m_scr, i32_scr):
    e_ts(P, f32_scr, src, scale_ap, 200.5, ALU.mult, ALU.add)
    e_copy(P, "vector", i32_scr, f32_scr)
    e_tt(P, m_scr, i32_scr, f32_scr, ALU.is_gt)
    e_stt(P, dst, i32_scr, 200.0, m_scr, ALU.subtract, ALU.subtract)
    e_stt(P, m_scr, i32_scr, 1.0, f32_scr, ALU.add, ALU.is_le)
    e_tt(P, dst, dst, m_scr, ALU.add)
    e_stt(P, m_scr, dst, 200.0, f32_scr, ALU.add, ALU.is_gt)
    e_tt(P, dst, dst, m_scr, ALU.subtract)
    e_stt(P, m_scr, dst, 201.0, f32_scr, ALU.add, ALU.is_le)
    e_tt(P, dst, dst, m_scr, ALU.add)


def e_rsqrt_newton(P, dst, u, scr, c0):
    s0 = scr[:, c0:c0 + 1]
    r0 = scr[:, c0 + 1:c0 + 2]
    t1 = scr[:, c0 + 2:c0 + 3]
    e_sqrt(P, s0, u)
    e_recip(P, r0, s0)
    e_tt(P, t1, r0, r0, ALU.mult)
    e_tt(P, t1, t1, u, ALU.mult)
    e_ts(P, t1, t1, -0.5, 1.5, ALU.mult, ALU.add)
    e_tt(P, dst, r0, t1, ALU.mult)


def e_sqrt_newton(P, dst, u, scr, c0):
    s0 = scr[:, c0:c0 + 1]
    r0 = scr[:, c0 + 1:c0 + 2]
    t1 = scr[:, c0 + 2:c0 + 3]
    e_sqrt(P, s0, u)
    e_ts(P, s0, s0, 1e-30, None, ALU.max)
    e_recip(P, r0, s0)
    e_tt(P, t1, u, r0, ALU.mult)
    e_tt(P, t1, t1, s0, ALU.add)
    e_ts(P, dst, t1, 0.5, None, ALU.mult)



def build_program(n_stages=None):
    if n_stages is None:
        n_stages = N_STAGES
    nc = bass.Bass("TRN2", target_bir_lowering=False)

    def din(name, shape, dtype=F32):
        return nc.dram_tensor(name, list(shape), dtype, kind="ExternalInput").ap()

    x_in = din("x", (S, D))
    out_t = nc.dram_tensor("out", [S, D], F32, kind="ExternalOutput").ap()

    wqkvT0 = din("wqkvT0", (D, 3 * D))
    wprojT0 = din("wprojT0", (D, D))
    w1T = din("w1T", (D, DFF))
    w2T = din("w2T", (DFF, D))
    b1bc = din("b1bc", (128, DFF))
    b2bc = din("b2bc", (128, D))
    wqkvT1 = din("wqkvT1", (D, 3 * D))
    bqkv1bc = din("bqkv1bc", (128, 3 * D))
    woT1 = din("woT1", (D, D))
    bo1bc = din("bo1bc", (128, D))
    wT_l1f = din("wT_l1f", (D, D))
    bl1f_bc = din("bl1f_bc", (128, D))
    lora_downT = din("lora_downT", (100, 4))
    lora_upT = din("lora_upT", (4, 50))
    down_aux = din("down_aux", (100, D))
    up_auxT = din("up_auxT", (50, D))
    identf = din("identf", (128, 128))
    identb = din("identb", (128, 128), BF16)
    cmask = din("cmask", (128, 128))
    ropeC = din("ropeC", (S, 32))
    ropeS = din("ropeS", (S, 32))
    tgtc = din("tgtc", (128, NT))
    iota8 = din("iota8", (128, 8))
    w1q_d = nc.dram_tensor("w1q_d", [D, DFF], BF16, kind="Internal").ap()
    w2q_d = nc.dram_tensor("w2q_d", [DFF, D], BF16, kind="Internal").ap()

    def tap(name, ap):
        if name not in TAPS:
            return
        t = nc.dram_tensor("tap_" + name, list(ap.shape), ap.dtype,
                           kind="ExternalOutput").ap()
        P.dma("dse", t, ap)

    es = ExitStack()
    with es:
        P = Prog(nc, es)
        sbg = lambda nm, shp, dt=F32: es.enter_context(nc.sbuf_tensor(nm, shp, dt))[:]

        X = []
        IDF = sbg("IDF", [128, 128])
        IDB = sbg("IDB", [128, 128], BF16)
        CMs = sbg("CMs", [128, 128])
        TCs = sbg("TCs", [128, NT])
        IO8 = sbg("IO8", [128, 8])
        GSM = sbg("GSM", [128, 64])

        if n_stages == 0:
            X.extend(sbg(f"X{i}", [128, D]) for i in range(NT))
            for i in range(NT):
                P.dma("dge", X[i], x_in[i * 128:(i + 1) * 128, :])
        P.dma("dsy", IDF, identf)
        P.dma("dsy", IDB, identb)
        P.dma("dsy", CMs, cmask)
        P.dma("dsy", TCs, tgtc)
        P.dma("dsy", IO8, iota8)
        P.flush_block()

        def attn_layer(layer):
            wqkvT = wqkvT0 if layer == 0 else wqkvT1
            wpT = wprojT0 if layer == 0 else woT1
            with ExitStack() as ls:
                sl_ = lambda nm, shp, dt=F32: ls.enter_context(
                    nc.sbuf_tensor(uname(nm), shp, dt, side="right"))[:]
                AVT = [sl_(f"AVT{c}", [128, S]) for c in range(8)]
                for hg in range(NHG):
                    with ExitStack() as hs:
                        s2 = lambda nm, shp, dt=F32: hs.enter_context(
                            nc.sbuf_tensor(uname(nm), shp, dt))[:]
                        p2 = lambda nm, shp: hs.enter_context(
                            nc.psum_tensor(uname(nm), shp, F32))[:]
                        WQ = [s2(f"WQ{p}", [128, 512]) for p in range(2)]
                        KT = [s2(f"KT{c}", [128, S]) for c in range(2)]
                        VS = [s2(f"VS{i}", [128, 256]) for i in range(NT)]
                        BQ = [s2(f"BQ{p}", [128, 256]) for p in range(2)] \
                            if layer == 1 else None
                        XTt = [[s2(f"XTt{t}_{f}", [128, 128]) for f in range(8)]
                               for t in range(2)]
                        QTt = [[s2(f"QTt{t}_{c}", [128, 128]) for c in range(2)]
                               for t in range(2)]
                        RW1 = s2("RW1", [128, 512])
                        QRr = s2("QRr", [128, 256])
                        RA = s2("RA", [128, 128])
                        RB = s2("RB", [128, 128])
                        RCB = [s2(f"RCB{t}", [128, 32]) for t in range(2)]
                        RSB = [s2(f"RSB{t}", [128, 32]) for t in range(2)]
                        if layer == 0:
                            SQs = s2("SQs", [128, 256])
                            NSM = s2("NSM", [128, 64])
                        nS = 4 if layer == 0 else 2
                        nP = 4 if layer == 0 else 1
                        Ssb = [s2(f"Ssb{p}", [128, S]) for p in range(nS)]
                        Psb = [s2(f"Psb{p}", [128, S]) for p in range(nP)]
                        PTT = [s2(f"PTT{p}", [128, 256 if layer == 0 else 128])
                               for p in range(2)]
                        STT = s2("STT", [128, 64])
                        if layer == 0:
                            J16 = s2("J16", [128, S], BF16)
                            JS16 = s2("JS16", [128, S], BF16)
                            M8 = s2("M8", [128, 64])
                            OH = s2("OH", [128, 64])
                        PPART = p2("PPART", [128, 1024])
                        PSs = [p2(f"PSs{p}", [128, 512]) for p in range(2)]
                        PTp = [p2(f"PTp{p}", [128, 128]) for p in range(2)]
                        PAV = [p2(f"PAV{p}", [64, 256]) for p in range(2)]

                        XIN = [s2(f"XIN{t}", [128, D]) for t in range(2)] \
                            if layer == 0 else None

                        def sbuf_idx(t, hl):
                            if layer == 0:
                                return t * 2 + (hl % 2)
                            return t

                        PENDAV = [None]
                        for ib in range(NT // 2):
                            tt2 = [ib * 2, ib * 2 + 1]
                            for t, i in enumerate(tt2):
                                P.dma("dge", RCB[t], ropeC[i * 128:(i + 1) * 128, :])
                                P.dma("dge", RSB[t], ropeS[i * 128:(i + 1) * 128, :])
                                if layer == 0:
                                    P.dma("dge", XIN[t],
                                          x_in[i * 128:(i + 1) * 128, :])
                                xi = XIN[t] if layer == 0 else X[i]
                                for f in range(8):
                                    e_trans(P, PTp[f % 2],
                                            xi[:, f * 128:(f + 1) * 128], IDF)
                                    e_copy(P, "scalar", XTt[t][f], PTp[f % 2])
                            c0q = hg * 256
                            c0k = D + hg * 256
                            for f in range(8):
                                wb = WQ[f % 2]
                                P.dma("dsy", wb[:, 0:256],
                                      wqkvT[f * 128:(f + 1) * 128, c0q:c0q + 256])
                                P.dma("dsy", wb[:, 256:512],
                                      wqkvT[f * 128:(f + 1) * 128, c0k:c0k + 256])
                                for t in range(2):
                                    e_mm(P, PPART[:, t * 512:(t + 1) * 512],
                                         XTt[t][f], wb, start=(f == 0), stop=(f == 7))
                            for t, i in enumerate(tt2):
                                for part in range(2):
                                    sl = PPART[:, t * 512 + part * 256:
                                               t * 512 + part * 256 + 256]
                                    e_copy(P, "scalar", RW1[:, part * 256:
                                                            part * 256 + 256], sl)
                                if layer == 1:
                                    P.dma("dsy", BQ[0], bqkv1bc[:, c0q:c0q + 256])
                                    P.dma("dsy", BQ[1], bqkv1bc[:, c0k:c0k + 256])
                                    e_tt(P, RW1[:, 0:256], RW1[:, 0:256],
                                         BQ[0], ALU.add)
                                    e_tt(P, RW1[:, 256:512], RW1[:, 256:512],
                                         BQ[1], ALU.add)
                                for part in range(2):
                                    rw = RW1[:, part * 256:part * 256 + 256]
                                    rope_apply(P, QRr, rw, RCB[t], RSB[t],
                                               HPG, RA, RB)
                                    if layer == 0:
                                        qk_unitnorm(P, QRr, HPG, NSM, SQs)
                                    for c in range(2):
                                        e_trans(P, PTp[c],
                                                QRr[:, c * 128:(c + 1) * 128], IDF)
                                        if part == 0:
                                            e_copy(P, "scalar", QTt[t][c], PTp[c])
                                        else:
                                            e_copy(P, "scalar",
                                                   KT[c][:, i * 128:(i + 1) * 128],
                                                   PTp[c])
                            c0v = 2 * D + hg * 256
                            for f in range(8):
                                wb = WQ[f % 2]
                                P.dma("dsy", wb[:, 0:256],
                                      wqkvT[f * 128:(f + 1) * 128, c0v:c0v + 256])
                                for t in range(2):
                                    e_mm(P, PPART[:, t * 512:t * 512 + 256],
                                         XTt[t][f], wb[:, 0:256],
                                         start=(f == 0), stop=(f == 7))
                            for t, i in enumerate(tt2):
                                e_copy(P, "scalar", VS[i],
                                       PPART[:, t * 512:t * 512 + 256])
                                if layer == 1:
                                    P.dma("dsy", BQ[t], bqkv1bc[:, c0v:c0v + 256])
                                    e_tt(P, VS[i], VS[i], BQ[t], ALU.add)

                            def mk_scores(t, i, hl):
                                si = sbuf_idx(t, hl)
                                K_i = (i + 1) * 128
                                ch, ro = hl // 2, 64 * (hl % 2)
                                qT = QTt[t][ch][ro:ro + 64, :]
                                nfull = (i * 128) // 512
                                blocks = [(b * 512, 512) for b in range(nfull)]
                                rem = (i * 128) % 512
                                if rem:
                                    blocks.append((nfull * 512, rem))
                                blocks.append((i * 128, 128))
                                for bi_, (bc0, bw) in enumerate(blocks):
                                    pb = PSs[bi_ % 2]
                                    e_mm(P, pb[:, 0:bw], qT,
                                         KT[ch][ro:ro + 64, bc0:bc0 + bw],
                                         start=True, stop=True)
                                    e_mul(P, Ssb[si][:, bc0:bc0 + bw],
                                          pb[:, 0:bw], 0.125)
                                e_tt(P, Ssb[si][:, i * 128:K_i],
                                     Ssb[si][:, i * 128:K_i], CMs, ALU.add)

                            def mk_softmax_av(t, i, hl, pslot):
                                si = sbuf_idx(t, hl)
                                K_i = (i + 1) * 128
                                col = t + 2 * (hl % 2)
                                z = STT[:, 24 + col:25 + col]
                                rz = STT[:, 28 + col:29 + col]
                                sA = Ssb[si][:, 0:K_i]
                                pb = (Psb[pslot][:, 0:K_i] if pslot >= 0
                                      else sA)
                                if layer == 0:
                                    hi = STT[:, 16 + col:17 + col]
                                    chi = STT[:, 20 + col:21 + col]
                                    e_ts(P, J16[:, 0:K_i], sA, hi, None,
                                         ALU.is_gt, ALU.add, accum=chi)
                                    e_ts(P, J16[:, 0:K_i], sA, hi, -1000.0,
                                         ALU.is_gt, ALU.mult)
                                    e_tt(P, pb, J16[:, 0:K_i], sA, ALU.add)
                                    e_max8(P, M8[:, col * 8:(col + 1) * 8], pb)
                                    idx = STT[:, 32 + col:33 + col]
                                    e_tt(P, idx, STT[:, 12 + t:13 + t], chi,
                                         ALU.subtract)
                                    e_ts(P, idx, idx, 1.0, 0.0,
                                         ALU.subtract, ALU.max)
                                    e_ts(P, idx, idx, 7.0, None, ALU.min)
                                    e_tt(P, OH[:, col * 8:(col + 1) * 8], IO8,
                                         idx.broadcast_to([128, 8]), ALU.is_equal)
                                    e_tt(P, OH[:, col * 8:(col + 1) * 8],
                                         OH[:, col * 8:(col + 1) * 8],
                                         M8[:, col * 8:(col + 1) * 8], ALU.mult)
                                    tau = STT[:, 36 + col:37 + col]
                                    e_red(P, tau, OH[:, col * 8:(col + 1) * 8],
                                          ALU.add)
                                    e_act(P, pb, sA, AFT.Exp)
                                    e_stt(P, pb, sA, tau, pb,
                                          ALU.is_ge, ALU.mult, accum=z)
                                    e_recip(P, rz, z)
                                    e_act(P, pb, pb, AFT.Copy, scale=rz)
                                    if hg == 0 and i == 1:
                                        tap(f"selstt{hl}", STT[:, 0:20])
                                        tap(f"selchi{hl}", STT[:, 20 + col:21 + col])
                                        tap(f"seltau{hl}", STT[:, 36 + col:37 + col])
                                        tap(f"selidx{hl}", STT[:, 32 + col:33 + col])
                                        tap(f"selm8{hl}", M8[:, col * 8:(col + 1) * 8])
                                        tap(f"selp{hl}", Psb[pslot][:, 0:256])
                                        tap(f"selsA{hl}", Ssb[si][:, 0:256])
                                else:
                                    e_act(P, pb, sA, AFT.Exp, accum=z)
                                    e_recip(P, rz, z)
                                    e_act(P, pb, pb, AFT.Copy, scale=rz)

                            def mk_av_g(hl, i0, i1):
                                pav = PAV[hl % 2]
                                pb0 = Psb[hl % 2]
                                pb1 = Psb[2 + (hl % 2)] if len(Psb) > 2 else Psb[1]
                                for c in range(i0 + 1):
                                    par = c % 2
                                    e_trans(P, PTp[par],
                                            pb0[:, c * 128:(c + 1) * 128], IDF)
                                    e_copy(P, "scalar", PTT[par][:, 0:128], PTp[par])
                                    e_trans(P, PTp[par],
                                            pb1[:, c * 128:(c + 1) * 128], IDF)
                                    e_copy(P, "scalar", PTT[par][:, 128:256], PTp[par])
                                    e_mm(P, pav, VS[c][:, hl * 64:(hl + 1) * 64],
                                         PTT[par], start=(c == 0), stop=False)
                                    yield
                                par = (i0 + 1) % 2
                                e_trans(P, PTp[par],
                                        pb1[:, i1 * 128:(i1 + 1) * 128], IDF)
                                e_copy(P, "scalar", PTT[par][:, 128:256], PTp[par])
                                e_mm(P, pav[:, 128:256],
                                     VS[i1][:, hl * 64:(hl + 1) * 64],
                                     PTT[par][:, 128:256], start=False, stop=True)
                                e_copy(P, "scalar",
                                       AVT[hg * 2 + hl // 2]
                                       [(hl % 2) * 64:(hl % 2) * 64 + 64,
                                        i0 * 128:i0 * 128 + 256], pav)

                            def mk_av_pair(hl, i0, i1):
                                for _ in mk_av_g(hl, i0, i1):
                                    pass

                            def mk_select_g(pr):
                                he, ho = 2 * pr, 2 * pr + 1
                                tv = STT[:, 0:4]
                                cn = STT[:, 4:8]
                                ge = STT[:, 8:12]
                                tg = STT[:, 12:16]
                                hi4 = STT[:, 16:20]
                                for t, i in enumerate(tt2):
                                    e_copy(P, "vector", tg[:, t:t + 1],
                                           TCs[:, i:i + 1])
                                    e_ts(P, tg[:, 2 + t:3 + t], TCs[:, i:i + 1],
                                         2.0, -float((i + 1) * 128),
                                         ALU.mult, ALU.add)
                                e_memset(P, tv, 0.0)
                                ntv = STT[:, 40:42]
                                delta = SEL_DELTA0
                                for r in range(R_SEL):
                                    delta *= 0.5
                                    e_ts(P, ntv, tv[:, 2:4], -1.0, None, ALU.mult)
                                    for t, i in enumerate(tt2):
                                        K_i = (i + 1) * 128
                                        sie = sbuf_idx(t, he)
                                        sio = sbuf_idx(t, ho)
                                        e_ts(P, J16[:, 0:K_i],
                                             Ssb[sie][:, 0:K_i],
                                             tv[:, t:t + 1], None,
                                             ALU.is_gt, ALU.add,
                                             accum=cn[:, t:t + 1])
                                        def em_sign(E, d=JS16[:, 0:K_i],
                                                    s=Ssb[sio][:, 0:K_i],
                                                    b=ntv[:, t:t + 1],
                                                    ac=cn[:, 2 + t:3 + t]):
                                            return E.activation(
                                                out=d, in_=s, func=AFT.Sign,
                                                bias=b, scale=1.0,
                                                accum_out=ac)
                                        P.op("scalar", em_sign,
                                             reads=[Ssb[sio], STT],
                                             writes=[JS16, STT])
                                    e_tt(P, ge, cn, tg, ALU.is_ge)
                                    e_stt(P, tv, ge, 2.0 * delta, tv,
                                          ALU.mult, ALU.add)
                                    e_ts(P, tv, tv, delta, None, ALU.subtract)
                                    yield
                                e_ts(P, hi4, tv, delta, None, ALU.add)

                            _SENT = object()

                            def drive(selg, fillers):
                                fl = list(fillers)
                                for _ in selg:
                                    for g in list(fl):
                                        if next(g, _SENT) is _SENT:
                                            fl.remove(g)
                                for g in fl:
                                    for _ in g:
                                        pass

                            i0, i1 = tt2[0], tt2[1]
                            if layer == 0:
                                for t, i in enumerate(tt2):
                                    mk_scores(t, i, 0)
                                    mk_scores(t, i, 1)
                                fillers0 = []
                                if PENDAV[0] is not None:
                                    pi0, pi1 = PENDAV[0]
                                    fillers0 = [mk_av_g(2, pi0, pi1),
                                                mk_av_g(3, pi0, pi1)]
                                    PENDAV[0] = None
                                drive(mk_select_g(0), fillers0)
                                mk_softmax_av(0, i0, 0, 0)
                                mk_softmax_av(1, i1, 0, 2)
                                mk_softmax_av(0, i0, 1, 1)
                                mk_softmax_av(1, i1, 1, 3)
                                mk_scores(0, i0, 2)
                                mk_scores(1, i1, 2)
                                mk_scores(0, i0, 3)
                                mk_scores(1, i1, 3)
                                drive(mk_select_g(1),
                                      [mk_av_g(0, i0, i1), mk_av_g(1, i0, i1)])
                                mk_softmax_av(0, i0, 2, 0)
                                mk_softmax_av(1, i1, 2, 2)
                                mk_softmax_av(0, i0, 3, 1)
                                mk_softmax_av(1, i1, 3, 3)
                                PENDAV[0] = (i0, i1)
                            else:
                                def mk_av1_g(t, i, hl):
                                    pb = (Psb[0] if t == 0
                                          else Ssb[sbuf_idx(1, hl)])
                                    for c in range(i + 1):
                                        par = (c + t) % 2
                                        e_trans(P, PTp[par],
                                                pb[:, c * 128:(c + 1) * 128], IDF)
                                        e_copy(P, "scalar", PTT[par], PTp[par])
                                        e_mm(P, PAV[t][:, 0:128],
                                             VS[c][:, hl * 64:(hl + 1) * 64],
                                             PTT[par], start=(c == 0), stop=(c == i))
                                        yield
                                    e_copy(P, "scalar",
                                           AVT[hg * 2 + hl // 2]
                                           [(hl % 2) * 64:(hl % 2) * 64 + 64,
                                            i * 128:(i + 1) * 128],
                                           PAV[t][:, 0:128])
                                for hl in range(HPG):
                                    for t, i in enumerate(tt2):
                                        mk_scores(t, i, hl)
                                    mk_softmax_av(0, i0, hl, 0)
                                    mk_softmax_av(1, i1, hl, -1)
                                    gs = [mk_av1_g(0, i0, hl),
                                          mk_av1_g(1, i1, hl)]
                                    while gs:
                                        gs = [g for g in gs
                                              if next(g, _SENT) is not _SENT]
                        if layer == 0 and PENDAV[0] is not None:
                            pi0, pi1 = PENDAV[0]
                            mk_av_pair(2, pi0, pi1)
                            mk_av_pair(3, pi0, pi1)
                            PENDAV[0] = None
                        P.flush_block()
                for c_ in range(8):
                    tap(f"avt{layer}_{c_}", AVT[c_][:, 128:256])
                if layer == 0 and not X:
                    X.extend(sbg(f"X{i}", [128, D]) for i in range(NT))
                with ExitStack() as pj:
                    s3 = lambda nm, shp, dt=F32: pj.enter_context(
                        nc.sbuf_tensor(uname(nm), shp, dt))[:]
                    WPJ = [s3(f"WPJ{f}", [128, D]) for f in range(8)]
                    BO = s3("BO", [128, D]) if layer == 1 else None
                    PPJ = pj.enter_context(
                        nc.psum_tensor(uname("PPJ"), [128, D], F32))[:]
                    for f in range(8):
                        P.dma("dsy", WPJ[f], wpT[f * 128:(f + 1) * 128, :])
                    if layer == 1:
                        P.dma("dsy", BO, bo1bc)
                    for i in range(NT):
                        if layer == 0:
                            P.dma("dge", X[i], x_in[i * 128:(i + 1) * 128, :])
                        for f in range(8):
                            for o2 in range(2):
                                e_mm(P, PPJ[:, o2 * 512:(o2 + 1) * 512],
                                     AVT[f][:, i * 128:(i + 1) * 128],
                                     WPJ[f][:, o2 * 512:(o2 + 1) * 512],
                                     start=(f == 0), stop=(f == 7))
                        e_tt(P, X[i], X[i], PPJ, ALU.add)
                        if layer == 1:
                            e_tt(P, X[i], X[i], BO, ALU.add)
                        layernorm_inplace(P, X[i], GSM)
                    P.flush_block()

        def l0_ffn():
            with ExitStack() as ph0:
                RSW = ph0.enter_context(nc.sbuf_tensor(uname("RSW"), [128, 2], F32))[:]
                l0_ffn_quant(RSW)
                l0_ffn_main(RSW)

        def l0_ffn_quant(RSW):
            with ExitStack() as qs:
                s2 = lambda nm, shp, dt=F32: qs.enter_context(nc.sbuf_tensor(uname(nm), shp, dt))[:]
                WS = [s2(f"WS{p}", [128, DFF]) for p in range(2)]
                WQB = [s2(f"WQB{p}", [128, DFF], BF16) for p in range(2)]
                WI = s2("WI", [128, 512], I32)
                RSC = s2("RSC", [128, 512])
                RSM = s2("RSM", [128, 512])
                CS = s2("CS", [128, 64])
                ONE = s2("ONE", [128, 1])
                ONER = s2("ONER", [1, 128])
                PS1 = qs.enter_context(nc.psum_tensor(uname("PS1"), [128, 8], F32))[:]
                e_memset(P, ONE, 1.0)
                e_memset(P, ONER, 1.0)
                for (wsrc, wdst, nchunk, wid, csl) in (
                        (w1T, w1q_d, 8, DFF, 0), (w2T, w2q_d, 32, D, 1)):
                    acc = CS[:, 4 + csl:5 + csl]
                    e_memset(P, acc, 0.0)
                    for f in range(nchunk):
                        p = f % 2
                        P.dma("dsy", WS[p][:, 0:wid], wsrc[f * 128:(f + 1) * 128, :])
                        nseg = wid // 128
                        e_red(P, CS[:, 20:20 + nseg],
                              WS[p][:, 0:wid].rearrange("p (g k) -> p g k", k=128),
                              ALU.add, absval=True)
                        e_red(P, CS[:, p:p + 1], CS[:, 20:20 + nseg], ALU.add)
                        e_tt(P, acc, acc, CS[:, p:p + 1], ALU.add)
                    e_mm(P, PS1[0:1, 0:1], ONE, acc, start=True, stop=True)
                    e_copy(P, "vector", CS[0:1, 8 + csl:9 + csl], PS1[0:1, 0:1])
                    e_ts(P, CS[0:1, 10 + csl:11 + csl], CS[0:1, 8 + csl:9 + csl],
                         1.0 / (D * DFF), 1e-5, ALU.mult, ALU.max)
                    e_recip(P, CS[0:1, 12 + csl:13 + csl], CS[0:1, 10 + csl:11 + csl])
                    e_mm(P, PS1[:, 1:2], ONER,
                         CS[0:1, 12 + csl:13 + csl], start=True, stop=True)
                    e_copy(P, "vector", CS[:, 14 + csl:15 + csl], PS1[:, 1:2])
                    e_mm(P, PS1[:, 2:3], ONER,
                         CS[0:1, 10 + csl:11 + csl], start=True, stop=True)
                    e_copy(P, "vector", CS[:, 16 + csl:17 + csl], PS1[:, 2:3])
                    sw = CS[:, 14 + csl:15 + csl]
                    for f in range(nchunk):
                        p = f % 2
                        P.dma("dsy", WS[p][:, 0:wid], wsrc[f * 128:(f + 1) * 128, :])
                        for sl in range(0, wid, 512):
                            round_halfup(P, WQB[p][:, sl:sl + 512],
                                         WS[p][:, sl:sl + 512], sw, RSC, RSM, WI)
                        e_ts(P, WQB[p][:, 0:wid], WQB[p][:, 0:wid], -1.0, 1.0,
                             ALU.max, ALU.min)
                        P.dma("dge", wdst[f * 128:(f + 1) * 128, :], WQB[p][:, 0:wid])
                e_copy(P, "vector", RSW[:, 0:1], CS[:, 16:17])
                e_copy(P, "vector", RSW[:, 1:2], CS[:, 17:18])
                tap("ffn_cs", CS)
                P.flush_block()
                tap("ffn_w1q", w1q_d[0:128, :])

        def l0_ffn_main(RSW):
                rsw1g = RSW[:, 0:1]
                rsw2g = RSW[:, 1:2]
                with ExitStack() as ms:
                    s3 = lambda nm, shp, dt=F32: ms.enter_context(
                        nc.sbuf_tensor(uname(nm), shp, dt))[:]
                    B1W = [s3(f"B1W{p}", [128, 512]) for p in range(2)]
                    B2s = s3("B2s", [128, D])
                    P.dma("dsy", B2s, b2bc)
                    W1B = [s3(f"W1B{p}", [128, 512], BF16) for p in range(2)]
                    W2B = [s3(f"W2B{p}", [128, 512], BF16) for p in range(2)]
                    XQT = [[s3(f"XQT{t}_{f}", [128, 128], BF16) for f in range(8)]
                           for t in range(4)]
                    HQT = [[s3(f"HQT{t}_{f}", [128, 128], BF16) for f in range(32)]
                           for t in range(4)]
                    HSB = [s3(f"HSB{t}", [128, DFF]) for t in range(4)]
                    XN = s3("XN", [128, D])
                    XQ = s3("XQ", [128, D], BF16)
                    HNS = s3("HNS", [128, 512])
                    GSC = s3("GSC", [128, 512])
                    RSC2 = s3("RSC2", [128, 512])
                    RSM2 = s3("RSM2", [128, 512])
                    HQs = s3("HQs", [128, 512], BF16)
                    WI2 = s3("WI2", [128, 512], I32)
                    QS1 = s3("QS1", [128, 8])
                    QS2 = s3("QS2", [128, 8])
                    SM = s3("SM", [128, 32])
                    SMB = s3("SMB", [128, 64])
                    PH = [ms.enter_context(nc.psum_tensor(uname(f"PH{t}"), [128, 512], F32))[:]
                          for t in range(4)]
                    PTq = [ms.enter_context(nc.psum_tensor(uname(f"PTq{p}"), [128, 128], BF16))[:]
                           for p in range(2)]
                    for tg in range(4):
                        tt4 = [tg * 4 + t for t in range(4)]
                        for t, i in enumerate(tt4):
                            e_tt(P, HSB[t][:, 0:D], X[i], X[i], ALU.mult)
                            e_red(P, SM[:, 0:1], HSB[t][:, 0:D], ALU.add)
                            e_ts(P, SM[:, 1:2], SM[:, 0:1], 1.0 / D, 1e-6,
                                 ALU.mult, ALU.add)
                            e_rsqrt_newton(P, SM[:, 3:4], SM[:, 1:2], SMB, 60)
                            e_ts(P, XN, X[i], SM[:, 3:4], None, ALU.mult)
                            e_red(P, SM[:, 4:5], XN, ALU.max, absval=True)
                            e_ts(P, SM[:, 5:6], SM[:, 4:5], 1e-5, None, ALU.max)
                            e_recip(P, SM[:, 6:7], SM[:, 5:6])
                            e_ts(P, SM[:, 7:8], SM[:, 6:7], 127.0, None, ALU.mult)
                            for sl in range(0, D, 512):
                                round_halfup(P, XQ[:, sl:sl + 512], XN[:, sl:sl + 512],
                                             SM[:, 7:8], RSC2, RSM2, WI2)
                            e_ts(P, SM[:, 8:9], SM[:, 5:6], 1.0 / 127.0, None, ALU.mult)
                            e_tt(P, QS1[:, t:t + 1], SM[:, 8:9], rsw1g, ALU.mult)
                            if i == 0:
                                tap("ffn_xq", XQ)
                                tap("ffn_xn", XN)
                                pass
                            for f in range(8):
                                e_trans(P, PTq[f % 2], XQ[:, f * 128:(f + 1) * 128], IDB)
                                e_copy(P, "scalar", XQT[t][f], PTq[f % 2])
                        for ow in range(8):
                            for f in range(8):
                                wb = W1B[f % 2]
                                P.dma("dsy", wb, w1q_d[f * 128:(f + 1) * 128,
                                                      ow * 512:(ow + 1) * 512])
                                for t in range(4):
                                    e_mm(P, PH[t], XQT[t][f], wb,
                                         start=(f == 0), stop=(f == 7))
                            P.dma("dse", B1W[ow % 2], b1bc[:, ow * 512:(ow + 1) * 512])
                            for t in range(4):
                                hsl = HSB[t][:, ow * 512:(ow + 1) * 512]
                                e_stt(P, hsl, PH[t],
                                      QS1[:, t:t + 1], B1W[ow % 2], ALU.mult, ALU.add)
                                e_act(P, hsl, hsl, AFT.Gelu)
                        if tg == 0:
                            tap("ffn_h0", HSB[0])
                            tap("ffn_qs1", QS1[:, 0:4])
                        for t, i in enumerate(tt4):
                            bn_meanvar(P, HSB[t], SMB, 50, 51)
                            e_tt(P, SM[:, 10:11], SMB[:, 50:51], SMB[:, 50:51], ALU.mult)
                            e_tt(P, SM[:, 11:12], SMB[:, 51:52], SM[:, 10:11], ALU.add)
                            e_ts(P, SM[:, 12:13], SM[:, 11:12], 1e-6, None, ALU.add)
                            e_rsqrt_newton(P, SM[:, 14:15], SM[:, 12:13], SMB, 60)
                            for sl in range(8):
                                e_ts(P, HNS, HSB[t][:, sl * 512:(sl + 1) * 512],
                                     SM[:, 14:15], None, ALU.mult)
                                e_red(P, SM[:, 16 + sl:17 + sl], HNS, ALU.max, absval=True)
                            e_red(P, SM[:, 24:25], SM[:, 16:24], ALU.max)
                            e_ts(P, SM[:, 25:26], SM[:, 24:25], 1e-5, None, ALU.max)
                            e_recip(P, SM[:, 26:27], SM[:, 25:26])
                            e_ts(P, SM[:, 27:28], SM[:, 26:27], 127.0, None, ALU.mult)
                            e_ts(P, SM[:, 28:29], SM[:, 25:26], 1.0 / 127.0, None, ALU.mult)
                            e_tt(P, QS2[:, t:t + 1], SM[:, 28:29], rsw2g, ALU.mult)
                            for sl in range(8):
                                e_ts(P, HNS, HSB[t][:, sl * 512:(sl + 1) * 512],
                                     SM[:, 14:15], None, ALU.mult)
                                round_ladder(P, HQs, HNS, SM[:, 27:28], RSC2, RSM2, WI2)
                                if tg == 0 and t == 0 and sl == 0:
                                    tap("ffn_hq0", HQs)
                                    tap("ffn_hn0", HNS)
                                    tap("ffn_s2", SM[:, 27:28])
                                for f2 in range(4):
                                    e_trans(P, PTq[f2 % 2], HQs[:, f2 * 128:(f2 + 1) * 128],
                                            IDB)
                                    e_copy(P, "scalar", HQT[t][sl * 4 + f2], PTq[f2 % 2])
                        for ow in range(2):
                            for f2 in range(32):
                                wb = W2B[f2 % 2]
                                P.dma("dsy", wb, w2q_d[f2 * 128:(f2 + 1) * 128,
                                                      ow * 512:(ow + 1) * 512])
                                for t in range(4):
                                    e_mm(P, PH[t], HQT[t][f2], wb,
                                         start=(f2 == 0), stop=(f2 == 31))
                            for t, i in enumerate(tt4):
                                e_stt(P, HSB[t][:, ow * 512:(ow + 1) * 512], PH[t],
                                      QS2[:, t:t + 1], B2s[:, ow * 512:(ow + 1) * 512],
                                      ALU.mult, ALU.add)
                        for t, i in enumerate(tt4):
                            e_tt(P, X[i], X[i], HSB[t][:, 0:D], ALU.add)
                            layernorm_inplace(P, X[i], GSM)
                    P.flush_block()

        def l1_ffn():
            with ExitStack() as ph:
                s2 = lambda nm, shp, dt=F32: ph.enter_context(nc.sbuf_tensor(uname(nm), shp, dt))[:]
                LDT = s2("LDT", [100, 4])
                LUT = s2("LUT", [4, 50])
                DAX = s2("DAX", [100, D])
                UAT = s2("UAT", [50, D])
                T1 = s2("T1", [4, D])
                T2 = s2("T2", [50, D])
                T3 = s2("T3", [128, D])
                WTOT = [s2(f"WTOT{f}", [128, D]) for f in range(8)]
                BL = s2("BL", [128, D])
                XTf = [s2(f"XTf{p}", [128, 128]) for p in range(2)]
                FO = s2("FO", [128, D])
                PL = ph.enter_context(nc.psum_tensor(uname("PL"), [128, D], F32))[:]
                PT2 = [ph.enter_context(nc.psum_tensor(uname(f"PT2{p}"), [128, 128], F32))[:]
                       for p in range(2)]
                P.dma("dsy", LDT, lora_downT)
                P.dma("dsy", LUT, lora_upT)
                P.dma("dsy", DAX, down_aux)
                P.dma("dsy", UAT, up_auxT)
                P.dma("dsy", BL, bl1f_bc)
                for f in range(8):
                    P.dma("dsy", WTOT[f], wT_l1f[f * 128:(f + 1) * 128, :])
                for o2 in range(2):
                    e_mm(P, PL[0:4, o2 * 512:(o2 + 1) * 512], LDT,
                         DAX[:, o2 * 512:(o2 + 1) * 512], start=True, stop=True)
                e_copy(P, "vector", T1, PL[0:4, :])
                for o2 in range(2):
                    e_mm(P, PL[0:50, o2 * 512:(o2 + 1) * 512], LUT,
                         T1[:, o2 * 512:(o2 + 1) * 512], start=True, stop=True)
                e_copy(P, "vector", T2, PL[0:50, :])
                for oc in range(8):
                    for o2 in range(2):
                        e_mm(P, PL[:, o2 * 512:(o2 + 1) * 512],
                             UAT[:, oc * 128:(oc + 1) * 128],
                             T2[:, o2 * 512:(o2 + 1) * 512], start=True, stop=True)
                    e_copy(P, "vector", T3, PL)
                    for f in range(8):
                        e_trans(P, PT2[f % 2], T3[:, f * 128:(f + 1) * 128], IDF)
                        e_tt(P, WTOT[f][:, oc * 128:(oc + 1) * 128],
                             WTOT[f][:, oc * 128:(oc + 1) * 128], PT2[f % 2], ALU.add)
                for i in range(NT):
                    for f in range(8):
                        e_trans(P, PT2[f % 2], X[i][:, f * 128:(f + 1) * 128], IDF)
                        e_copy(P, "scalar", XTf[f % 2], PT2[f % 2])
                        for o2 in range(2):
                            e_mm(P, PL[:, o2 * 512:(o2 + 1) * 512], XTf[f % 2],
                                 WTOT[f][:, o2 * 512:(o2 + 1) * 512],
                                 start=(f == 0), stop=(f == 7))
                    e_tt(P, FO, PL, BL, ALU.add)
                    e_tt(P, X[i], X[i], FO, ALU.add)
                    layernorm_inplace(P, X[i], GSM)
                P.flush_block()

        if n_stages >= 1:
            attn_layer(0)
        if n_stages >= 2:
            l0_ffn()
        if n_stages >= 3:
            attn_layer(1)
        if n_stages >= 4:
            l1_ffn()

        for i in range(NT):
            P.dma("dge", out_t[i * 128:(i + 1) * 128, :], X[i])
        P.wait_all_dma("gpsimd")
        P.flush_block()

    return nc



def _host_consts():
    c = {}
    c["identf"] = np.eye(128, dtype=np.float32)
    c["identb"] = np.eye(128, dtype=np.float32).astype(ml_dtypes.bfloat16)
    m = np.zeros((128, 128), np.float32)
    m[np.triu_indices(128, 1)] = NEG
    c["cmask"] = m
    import jax
    import jax.numpy as jnp
    with jax.default_device(jax.devices("cpu")[0]):
        inv = 1.0 / 10000.0 ** (jnp.arange(0, HD, 2, dtype=jnp.float32) / HD)
        tpos = jnp.arange(S, dtype=jnp.float32)
        fr = jnp.einsum("i,j->ij", tpos, inv)
        c["ropeC"] = np.asarray(jnp.cos(fr), dtype=np.float32)
        c["ropeS"] = np.asarray(jnp.sin(fr), dtype=np.float32)
    q = (np.arange(NT)[None, :] * 128 + np.arange(128)[:, None]).astype(np.float64)
    c["tgtc"] = np.minimum(q + 1.0, 128.0).astype(np.float32)
    c["iota8"] = np.broadcast_to(np.arange(8, dtype=np.float32), (128, 8)).copy()
    return c


def _host_tensors(inputs):
    T = {}
    f = np.float32
    cc = np.ascontiguousarray
    T["wqkvT0"] = cc(np.asarray(inputs["L0_attn_w_qkv"]).T.astype(f))
    T["wprojT0"] = cc(np.asarray(inputs["L0_attn_w_proj"]).T.astype(f))
    T["w1T"] = cc(np.asarray(inputs["L0_ffn_w1"]).T.astype(f))
    T["w2T"] = cc(np.asarray(inputs["L0_ffn_w2"]).T.astype(f))
    T["b1bc"] = cc(np.broadcast_to(np.asarray(inputs["L0_ffn_b1"]).astype(f), (128, DFF)))
    T["b2bc"] = cc(np.broadcast_to(np.asarray(inputs["L0_ffn_b2"]).astype(f), (128, D)))
    T["wqkvT1"] = cc(np.concatenate(
        [np.asarray(inputs["L1_attn_Wq"]).T, np.asarray(inputs["L1_attn_Wk"]).T,
         np.asarray(inputs["L1_attn_Wv"]).T], axis=1).astype(f))
    T["bqkv1bc"] = cc(np.broadcast_to(np.concatenate(
        [np.asarray(inputs["L1_attn_bq"]), np.asarray(inputs["L1_attn_bk"]),
         np.asarray(inputs["L1_attn_bv"])]).astype(f), (128, 3 * D)))
    T["woT1"] = cc(np.asarray(inputs["L1_attn_Wo"]).T.astype(f))
    T["bo1bc"] = cc(np.broadcast_to(np.asarray(inputs["L1_attn_bo"]).astype(f), (128, D)))
    T["wT_l1f"] = cc(np.asarray(inputs["L1_ffn_W"]).T.astype(f))
    T["bl1f_bc"] = cc(np.broadcast_to(np.asarray(inputs["L1_ffn_b"]).astype(f), (128, D)))
    T["lora_downT"] = cc(np.asarray(inputs["L1_ffn_lora_down"]).T.astype(f))
    T["lora_upT"] = cc(np.asarray(inputs["L1_ffn_lora_up"]).T.astype(f))
    T["down_aux"] = cc(np.asarray(inputs["L1_ffn_down_aux"]).astype(f))
    T["up_auxT"] = cc(np.asarray(inputs["L1_ffn_up_aux"]).T.astype(f))
    T.update(_host_consts())
    return T


def kernel(**inputs):
    x = np.asarray(inputs["L0_attn_x"], dtype=np.float32)
    T = _host_tensors(inputs)
    nc = build_program(N_STAGES)
    in_maps = []
    for c in range(N_CORES):
        m = dict(T)
        m["x"] = np.ascontiguousarray(x[c])
        in_maps.append(m)
    res = run_bass_kernel_spmd(nc, in_maps, core_ids=list(range(N_CORES)))
    return np.stack([r["out"] for r in res.results], axis=0)
